# version 08
# baseline (speedup 1.0000x reference)
import contextlib
import os as _os
import numpy as np
import ml_dtypes
import concourse.bass as bass
import concourse.mybir as mybir
from concourse.bass_utils import run_bass_kernel_spmd

F32 = mybir.dt.float32
BF16 = mybir.dt.bfloat16
U32 = mybir.dt.uint32
I32d = mybir.dt.int32
AF = mybir.ActivationFunctionType
ALU = mybir.AluOpType
AX = mybir.AxisListType

B, T, D = 4, 8192, 1024
HL = 8
II = 2816
TH = T // 2
NT1 = T // 128
NT2 = TH // 128
NIC = II // 128
R23 = float(2 ** 23)
CHI = 127.499
CLO = -128.499
AGB = 36864
NCORES = 8


class Op:
    __slots__ = ("eng", "fn", "deps", "seq", "drain")

    def __init__(self, eng, fn, deps):
        self.eng, self.fn, self.deps = eng, fn, list(deps)
        self.seq = None


class Prog:
    PHYS = {"v": "vector", "s": "scalar", "t": "tensor", "g": "gpsimd",
            "sq": "sync", "aq": "scalar", "gq": "gpsimd"}
    AMT = {"v": 1, "s": 1, "t": 1, "g": 1, "sq": 16, "aq": 16, "gq": 16}

    def __init__(self, nc, es):
        self.nc = nc
        self.ops = []
        self.sems = {k: es.enter_context(nc.semaphore(f"sem_{k}")) for k in self.PHYS}
        self.cum = {k: 0 for k in self.PHYS}
        self.barrier_ops = None
        self.barrier_seen = set()

    def op(self, eng, deps, fn, n_dma=1, drain=True):
        deps = [d for d in deps if d is not None]
        if self.barrier_ops is not None and eng not in self.barrier_seen:
            deps = deps + self.barrier_ops
            self.barrier_seen.add(eng)
        o = Op(eng, fn, deps)
        o.drain = drain
        self.cum[eng] += self.AMT[eng] * (n_dma if eng in ("sq", "aq", "gq") else 1)
        o.seq = self.cum[eng]
        self.ops.append(o)
        return o

    def barrier(self):
        last = {}
        for o in self.ops:
            last[o.eng] = o
        self.barrier_ops = list(last.values())
        self.barrier_seen = set()

    def finalize(self, block):
        streams = {}
        for o in self.ops:
            streams.setdefault(self.PHYS[o.eng], []).append(o)
        for phys in ("sync", "vector", "scalar", "tensor", "gpsimd"):
            ops = streams.get(phys)
            if not ops:
                continue

            def emit(e, ops=ops):
                for o in ops:
                    waited = {}
                    for d in o.deps:
                        if d.eng == o.eng:
                            continue
                        key = d.eng
                        waited[key] = max(waited.get(key, 0), d.seq)
                    for k, v in waited.items():
                        e.wait_ge(self.sems[k], v)
                    last = o.fn(e)
                    if o.eng in ("v", "s") and o.drain:
                        last = e.drain()
                    last.then_inc(self.sems[o.eng], self.AMT[o.eng])
            getattr(block, phys)(emit)


def build_nc(debug=()):
    _MAXP = int(_os.environ.get("MK_MAXPHASE", "9"))
    nc = bass.Bass("TRN2", target_bir_lowering=False)

    def dram(name, shp, dtype):
        k = "ExternalOutput" if name in debug else "Internal"
        return nc.dram_tensor(name, shp, dtype, kind=k).ap()

    din = lambda name, shp, dtype: nc.dram_tensor(name, shp, dtype, kind="ExternalInput").ap()
    x_full = din("x_full", [T, D], F32)
    x_half = din("x_half", [TH, D], F32)
    wq_i = din("wq", [D, 512], F32)
    wk_i = din("wk", [D, 512], F32)
    wv_i = din("wv", [D, 512], F32)
    wo_i = din("wo", [D, D], F32)
    bq_r = din("bq_r", [128, 512], F32)
    bv_r = din("bv_r", [128, 512], F32)
    bk_c = din("bk_c", [128, 4], F32)
    bo_b = din("bo_b", [1, D], F32)
    wg_i = din("wg", [II, D], F32)
    wu_i = din("wu", [II, D], F32)
    wd_i = din("wd", [D, II], F32)
    clip_r = din("clip_r", [128, 2], F32)
    iota_f = din("iota_f", [128, T], F32)
    iota_loc = din("iota_loc", [128, TH], F32)
    ident_b = din("ident_b", [128, 128], BF16)
    ident_f = din("ident_f", [128, 128], F32)
    ones_b = din("ones_b", [1, 128], BF16)
    y = nc.dram_tensor("y", [TH, D], F32, kind="ExternalOutput").ap()
    xln_d = dram("xln_d", [T, D], BF16)
    q_d = dram("q_d", [T, 512], BF16)
    k_d = dram("k_d", [512, T], BF16)
    v_d = dram("v_d", [T, 512], BF16)
    qn_d = dram("qn_d", [HL, T], F32)
    idx_d = dram("idx_d", [HL, 48], U32)
    x1_d = dram("x1_d", [TH, D], F32)
    xln2_d = dram("xln2_d", [TH, D], F32)
    xq_d = dram("xq_d", [TH, D], BF16)
    s_d = dram("s_d", [II, TH], BF16)
    wgq_d = dram("wgq_d", [II, D], BF16)
    wuq_d = dram("wuq_d", [II, D], BF16)
    wdq_d = dram("wdq_d", [D, II], BF16)
    ag_in = dram("ag_in", [1, AGB], F32)
    ag_out = dram("ag_out", [2, AGB], F32)
    mx1_in = dram("mx1_in", [1, 2048], F32)
    mx1_out = dram("mx1_out", [1, 2048], F32)
    mx2_in = dram("mx2_in", [1, 2048], F32)
    mx2_out = dram("mx2_out", [1, 2048], F32)
    hq_dbg = dram("hq_dbg", [II, 512], BF16)
    wdt_dbg = dram("wdt_dbg", [II, D], BF16)
    sh_dbg = dram("sh_dbg", [128, 2], F32)

    g_es = contextlib.ExitStack()
    GSB = lambda n, s, d: g_es.enter_context(nc.sbuf_tensor(n, s, d))
    with g_es:
        P = Prog(nc, g_es)
        eps_s = GSB("eps_s", [128, 1], F32)
        lnt = GSB("lnt", [128, 8], F32)
        identb_s = GSB("identb_s", [128, 128], BF16)
        identf_s = GSB("identf_s", [128, 128], F32)
        onesb_s = GSB("onesb_s", [1, 128], BF16)
        clip_s = GSB("clip_s", [128, 2], F32)
        qn_tm = GSB("qn_tm", [128, NT1 * 8], F32)
        tidx = GSB("tidx", [HL, 48], U32)
        idxp_i = GSB("idxp_i", [128, 4], U32)
        idxp_f = GSB("idxp_f", [128, 4], F32)
        sgw = GSB("sgw", [128, NIC], F32)
        suw = GSB("suw", [128, NIC], F32)
        sdw = GSB("sdw", [128, 8], F32)
        rsx = GSB("rsx", [128, 1], F32)
        rsh = GSB("rsh", [128, 1], F32)
        shv = GSB("shv", [128, 1], F32)
        sxsg = GSB("sxsg", [128, NIC], F32)
        sxsu = GSB("sxsu", [128, NIC], F32)
        mxg = GSB("mxg", [128, 16], F32)
        mxrow = GSB("mxrow", [1, 2048], F32)
        cmx = GSB("cmx", [128, NT2], F32)
        bo_s = GSB("bo_s", [1, D], BF16)
        scr = GSB("scr", [128, 4], F32)
        so = [GSB(f"so{p}", [128, 64], F32) for p in range(4)]

        block = g_es.enter_context(nc.Block())

        ld_identb = P.op("sq", [], lambda e: e.dma_start(out=identb_s[:], in_=ident_b[:, :]))
        ld_identf = P.op("sq", [], lambda e: e.dma_start(out=identf_s[:], in_=ident_f[:, :]))
        ld_ones = P.op("sq", [], lambda e: e.dma_start(out=onesb_s[:], in_=ones_b[:, :]))
        ld_clip = P.op("sq", [], lambda e: e.dma_start(out=clip_s[:], in_=clip_r[:, :]))
        ld_bo = P.op("gq", [], lambda e: e.dma_start(out=bo_s[:], in_=bo_b[:, :]))
        op_eps = P.op("v", [], lambda e: e.memset(eps_s[:], 1e-5))

        p_es = contextlib.ExitStack()
        PSB = lambda n, s, d: p_es.enter_context(nc.sbuf_tensor(n, s, d))
        PPS = lambda n, s, d: p_es.enter_context(nc.psum_tensor(n, s, d))
        with p_es:
            xab = [PSB(f"x_{i}", [128, D], F32) for i in range(2)]
            junk = PSB("junk", [128, D], F32)
            lnt2c = [PSB(f"lnt2c{i}", [128, 8], F32) for i in range(2)]
            xlnb = [PSB(f"xln_{i}", [128, D], BF16) for i in range(2)]
            bq_s = PSB("bq_s", [128, 512], F32)
            bv_s = PSB("bv_s", [128, 512], F32)
            bk_s = PSB("bk_s", [128, 4], F32)
            wq_s = PSB("wq_s", [128, 8 * 512], BF16)
            wk_s = PSB("wk_s", [128, 8 * 512], BF16)
            wv_s = PSB("wv_s", [128, 8 * 512], BF16)
            xfm = [PSB(f"xfm{i}", [128, 2 * 1024], BF16) for i in range(8)]
            qt2 = [PSB(f"qt_{i}", [128, 512], BF16) for i in range(2)]
            vt2 = [PSB(f"vt_{i}", [128, 512], BF16) for i in range(2)]
            kt2 = [PSB(f"kt_{i}", [128, 512], BF16) for i in range(2)]
            wqb = [PSB(f"wq_{i}", [128, II], F32) for i in range(2)]
            wtmp = PSB("wtmp", [128, II], F32)
            wtmp2 = PSB("wtmp2", [128, II], I32d)
            wout = PSB("wout", [128, II], BF16)
            wmax = PSB("wmax", [128, 1], F32)
            wrec = PSB("wrec", [128, 1], F32)
            psQ = [PPS(f"psQ{i}", [128, 512], F32) for i in range(2)]
            psV = [PPS(f"psV{i}", [128, 512], F32) for i in range(2)]
            psK = [PPS(f"psK{i}", [128, 512], F32) for i in range(2)]

            ld_b = [P.op("sq", [], lambda e, s=s, d=d: e.dma_start(out=d[:], in_=s[:, :]))
                    for s, d in [(bq_r, bq_s), (bv_r, bv_s), (bk_c, bk_s)]]
            ld_w = []
            for wsrc, wdst in [(wq_i, wq_s), (wk_i, wk_s), (wv_i, wv_s)]:
                for dc in range(8):
                    ld_w.append(P.op("gq", [], lambda e, s=wsrc, d=wdst, dc=dc:
                                     e.dma_start(out=d[:, dc * 512:(dc + 1) * 512],
                                                 in_=s[dc * 128:(dc + 1) * 128, :])))
            lw_q, lw_k, lw_v = ld_w[7], ld_w[15], ld_w[23]

            NG, GT = 8, 1024
            ln_done = [None] * NT1
            ln_store = [None] * NT1
            ln_st1 = [None] * NT1
            qv_mm = [None] * NT1
            qv_ev = [None] * NT1
            qv_st = [None] * NT1
            k_mm = [None] * NT1
            k_ev = [None] * NT1
            k_st = [None] * NT1
            group_last_t = [None] * NG
            xfm_lds = [None] * NG

            ln_sq2 = [None] * NT1

            def emit_ln_front(t):
                buf = xab[t % 2]
                ld = P.op("sq", [ln_done[t - 2] if t >= 2 else None],
                          lambda e, t=t, buf=buf:
                          e.dma_start(out=buf[:], in_=x_full[t * 128:(t + 1) * 128, :]))
                sq1 = P.op("s", [ld, ln_st1[t - 1] if t >= 1 else None],
                           lambda e, buf=buf:
                           e.activation(out=junk[:], in_=buf[:], func=AF.Square,
                                        accum_out=lnt2c[t % 2][:, 0:1]))
                st1 = P.op("v", [ld, op_eps, sq1], lambda e, buf=buf, t=t: (
                    e.tensor_reduce(out=lnt2c[t % 2][:, 1:2], in_=buf[:], axis=AX.X,
                                    op=ALU.add),
                    e.drain(),
                    e.tensor_scalar(lnt2c[t % 2][:, 2:3], lnt2c[t % 2][:, 1:2], 1.0 / D,
                                    None, op0=ALU.mult),
                    e.drain(),
                    e.scalar_tensor_tensor(out=lnt2c[t % 2][:, 5:6],
                                           in0=lnt2c[t % 2][:, 2:3],
                                           scalar=lnt2c[t % 2][:, 2:3], in1=eps_s[:],
                                           op0=ALU.mult, op1=ALU.subtract),
                    e.drain(),
                    e.tensor_scalar(lnt2c[t % 2][:, 3:4], lnt2c[t % 2][:, 0:1], 1.0 / D,
                                    lnt2c[t % 2][:, 5:6],
                                    op0=ALU.mult, op1=ALU.subtract))[-1])
                ln_st1[t] = st1
                ln_sq2[t] = P.op("s", [st1], lambda e, t=t:
                                 e.activation(out=lnt2c[t % 2][:, 4:5],
                                              in_=lnt2c[t % 2][:, 3:4], func=AF.Sqrt))

            def emit_ln_back(t):
                buf, xout = xab[t % 2], xlnb[t % 2]
                ln_done[t] = P.op("v", [ln_sq2[t], ln_store[t - 2] if t >= 2 else None],
                                  lambda e, buf=buf, xout=xout, t=t: (
                    e.reciprocal(out=lnt2c[t % 2][:, 6:7], in_=lnt2c[t % 2][:, 4:5]),
                    e.drain(),
                    e.tensor_scalar(lnt2c[t % 2][:, 7:8], lnt2c[t % 2][:, 2:3],
                                    lnt2c[t % 2][:, 6:7], -1.0,
                                    op0=ALU.mult, op1=ALU.mult),
                    e.drain(),
                    e.scalar_tensor_tensor(out=xout[:], in0=buf[:],
                                           scalar=lnt2c[t % 2][:, 6:7],
                                           in1=lnt2c[t % 2][:, 7:8].to_broadcast([128, D]),
                                           op0=ALU.mult, op1=ALU.add))[-1])
                ln_store[t] = P.op("gq", [ln_done[t]], lambda e, t=t, xout=xout:
                                   e.dma_start(out=xln_d[t * 128:(t + 1) * 128, :],
                                               in_=xout[:]))

            def emit_xfm(g):
                par = g % 2
                lds = []
                for st in range(8):
                    deps = [ln_store[g * 8 + 7]]
                    if g >= 2:
                        deps.append(group_last_t[g - 2])
                    lds.append(P.op("aq", deps, lambda e, g=g, st=st, par=par:
                                    e.dma_start(out=xfm[st][:, par * 1024:(par + 1) * 1024],
                                                in_=xln_d[g * GT:(g + 1) * GT,
                                                          st * 128:(st + 1) * 128],
                                                transpose=True)))
                xfm_lds[g] = lds

            def emit_qkv(g):
                par = g % 2
                for tb in range(8):
                    j = g * 8 + tb
                    deps = [xfm_lds[g][-1], lw_q, lw_v]
                    if j >= 2:
                        deps.append(qv_ev[j - 2])

                    def mk_qv(tb=tb, j=j, par=par):
                        def f(e):
                            sl = slice(par * 1024 + tb * 128, par * 1024 + (tb + 1) * 128)
                            for dc in range(8):
                                e.matmul(psQ[j % 2][:, :], xfm[dc][:, sl],
                                         wq_s[:, dc * 512:(dc + 1) * 512],
                                         start=(dc == 0), stop=(dc == 7))
                            for dc in range(8):
                                mm = e.matmul(psV[j % 2][:, :], xfm[dc][:, sl],
                                              wv_s[:, dc * 512:(dc + 1) * 512],
                                              start=(dc == 0), stop=(dc == 7))
                            return mm
                        return f
                    qv_mm[j] = P.op("t", deps, mk_qv())
                    deps_e = [qv_mm[j], ld_b[0], ld_b[1]]
                    if j >= 2:
                        deps_e.append(qv_st[j - 2])

                    def mk_qvev(j=j):
                        def f(e):
                            e.tensor_add(qt2[j % 2][:], psQ[j % 2][:, :], bq_s[:])
                            e.tensor_add(vt2[j % 2][:], psV[j % 2][:, :], bv_s[:])
                            e.drain()
                            return e.tensor_reduce(
                                out=qn_tm[:, j * 8:(j + 1) * 8],
                                in_=qt2[j % 2][:].rearrange("p (h d) -> p h d", h=8),
                                axis=AX.X, op=ALU.add, apply_absolute_value=True)
                        return f
                    qv_ev[j] = P.op("v", deps_e, mk_qvev())

                    def mk_qvst(j=j, g=g, tb=tb):
                        def f(e):
                            r0 = g * GT + tb * 128
                            e.dma_start(out=q_d[r0:r0 + 128, :], in_=qt2[j % 2][:]
                                        ).then_inc(P.sems["gq"], 16)
                            return e.dma_start(out=v_d[r0:r0 + 128, :], in_=vt2[j % 2][:])
                        return f
                    qv_st[j] = P.op("gq", [qv_ev[j]], mk_qvst(), n_dma=2)
                for kk in range(8):
                    kj = g * 8 + kk
                    mc, ts2 = kk // 2, kk % 2
                    deps = [xfm_lds[g][-1], lw_k]
                    if kj >= 2:
                        deps.append(k_ev[kj - 2])

                    def mk_k(mc=mc, ts2=ts2, kj=kj, par=par):
                        def f(e):
                            sl = slice(par * 1024 + ts2 * 512, par * 1024 + (ts2 + 1) * 512)
                            for dc in range(8):
                                mm = e.matmul(psK[kj % 2][:, :],
                                              wk_s[:, dc * 512 + mc * 128:
                                                   dc * 512 + (mc + 1) * 128],
                                              xfm[dc][:, sl], start=(dc == 0), stop=(dc == 7))
                            return mm
                        return f
                    k_mm[kj] = P.op("t", deps, mk_k())
                    deps_e = [k_mm[kj], ld_b[2]]
                    if kj >= 2:
                        deps_e.append(k_st[kj - 2])
                    k_ev[kj] = P.op("v", deps_e, lambda e, kj=kj, mc=mc:
                                    e.tensor_scalar(kt2[kj % 2][:], psK[kj % 2][:, :],
                                                    bk_s[:, mc:mc + 1], None, op0=ALU.add))
                    k_st[kj] = P.op("gq", [k_ev[kj]], lambda e, kj=kj, g=g, mc=mc, ts2=ts2:
                                    e.dma_start(
                                        out=k_d[mc * 128:(mc + 1) * 128,
                                                g * GT + ts2 * 512:g * GT + ts2 * 512 + 512],
                                        in_=kt2[kj % 2][:]))
                group_last_t[g] = k_mm[g * 8 + 7]

            wq_items = ([("wg", ic) for ic in range(NIC)] +
                        [("wu", ic) for ic in range(NIC)] +
                        [("wd", hc) for hc in range(8)])
            wq_st = [None] * len(wq_items)
            wq_v = [None] * len(wq_items)

            def emit_wq(n):
                kind, ic = wq_items[n]
                buf = wqb[n % 2]
                cols = D if kind != "wd" else II
                src2 = {"wg": wg_i, "wu": wu_i, "wd": wd_i}[kind]
                ld = P.op("sq", [wq_v[n - 2] if n >= 2 else None],
                          lambda e, buf=buf, src2=src2, ic=ic, cols=cols:
                          e.dma_start(out=buf[:, 0:cols],
                                      in_=src2[ic * 128:(ic + 1) * 128, :]))
                deps_v = [ld]
                if n >= 1:
                    deps_v.append(wq_st[n - 1])

                def mk_wqv(kind=kind, ic=ic, cols=cols, buf=buf):
                    def f(e):
                        e.tensor_reduce(out=wmax[:], in_=buf[:, 0:cols], axis=AX.X,
                                        op=ALU.max, apply_absolute_value=True)
                        e.drain()
                        e.reciprocal(out=wrec[:], in_=wmax[:])
                        e.drain()
                        tgt = {"wg": sgw, "wu": suw, "wd": sdw}[kind]
                        e.tensor_scalar(tgt[:, ic:ic + 1], wmax[:], 1.0 / 127.0, None,
                                        op0=ALU.mult)
                        e.tensor_scalar(wtmp[:, 0:cols], buf[:, 0:cols], wrec[:], 127.0,
                                        op0=ALU.mult, op1=ALU.mult)
                        e.drain()
                        e.tensor_copy(out=wtmp2[:, 0:cols], in_=wtmp[:, 0:cols])
                        e.drain()
                        if kind != "wd":
                            return e.tensor_copy(out=wout[:, 0:cols], in_=wtmp2[:, 0:cols])
                        return e.tensor_scalar(wout[:, 0:cols], wtmp2[:, 0:cols],
                                               sdw[:, ic:ic + 1], None, op0=ALU.mult)
                    return f
                wq_v[n] = P.op("v", deps_v, mk_wqv())
                dstm = {"wg": wgq_d, "wu": wuq_d, "wd": wdq_d}[kind]
                wq_st[n] = P.op("gq", [wq_v[n]], lambda e, d=dstm, ic=ic, cols=cols:
                                e.dma_start(out=d[ic * 128:(ic + 1) * 128, :],
                                            in_=wout[:, 0:cols]))

            wq_n = 0
            for g in range(NG):
                for t in range(g * 8, g * 8 + 8):
                    emit_ln_front(t)
                    if t >= 1 and t % 8 != 0:
                        emit_ln_back(t - 1)
                emit_ln_back(g * 8 + 7)
                emit_xfm(g)
                if g >= 1:
                    emit_qkv(g - 1)
                    take = min(8, len(wq_items) - wq_n)
                    for _ in range(take):
                        emit_wq(wq_n)
                        wq_n += 1
            emit_qkv(NG - 1)
            while wq_n < len(wq_items):
                emit_wq(wq_n)
                wq_n += 1
            wg_stored, wu_stored, wd_stored = wq_st[NIC - 1], wq_st[2 * NIC - 1], wq_st[-1]

        if _MAXP < 3:
            P.op("v", [qv_ev[NT1 - 1], wq_v[-1], k_ev[NT1 - 1]], lambda e: e.nop())
            P.finalize(block)
            return nc
        P.barrier()
        p_es = contextlib.ExitStack()
        PSB = lambda n, s, d: p_es.enter_context(nc.sbuf_tensor(n, s, d))
        PPS = lambda n, s, d: p_es.enter_context(nc.psum_tensor(n, s, d))
        with p_es:
            qn_tt = PSB("qn_tt", [128, 512], F32)
            qn_hm = PSB("qn_hm", [HL, T], F32)
            mx8 = PSB("mx8", [HL, 8], F32)
            psTf = PPS("psTf", [128, 128], F32)

            tk_ev = [None] * 4
            tk_st = [None] * 4
            for jb in range(4):
                tr = P.op("t", [qv_ev[NT1 - 1], ld_identf,
                                tk_ev[jb - 1] if jb >= 1 else None],
                          lambda e, jb=jb:
                          e.transpose(psTf[:], qn_tm[:, jb * 128:(jb + 1) * 128], identf_s[:]))
                tk_ev[jb] = P.op("v", [tr], lambda e, jb=jb:
                                 e.tensor_copy(out=qn_tt[:, jb * 128:(jb + 1) * 128],
                                               in_=psTf[:]))
                tk_st[jb] = P.op("gq", [tk_ev[jb]], lambda e, jb=jb: e.dma_start(
                    out=qn_d[:, jb * 2048:(jb + 1) * 2048].rearrange("h (a p) -> a h p", p=128),
                    in_=qn_tt[:, jb * 128:(jb + 1) * 128]))
            tk_ld = P.op("sq", tk_st, lambda e: e.dma_start(out=qn_hm[:], in_=qn_d[:, :]))

            def topk_fn(e):
                for r in range(6):
                    e.max(out=mx8[:], in_=qn_hm[:])
                    e.drain()
                    if r == 5:
                        e.memset(mx8[:, 5:8], -1e30)
                        e.drain()
                    e.max_index(out=tidx[:, r * 8:(r + 1) * 8], in_max=mx8[:],
                                in_values=qn_hm[:])
                    e.drain()
                    e.match_replace(out=qn_hm[:], in_to_replace=mx8[:], in_values=qn_hm[:],
                                    imm_value=-1e30)
                    e.drain()
                e.memset(tidx[:, 45:48], 10000000)
                e.drain()
                return e.memset(idxp_i[:], 10000000)
            tk_main = P.op("v", [tk_ld], topk_fn)
            tk_idx_st = P.op("gq", [tk_main], lambda e: e.dma_start(out=idx_d[:, :], in_=tidx[:]))

            def idx_ld_fn(e):
                mms = []
                for p in range(4):
                    for hh in range(2):
                        h = 2 * p + hh
                        mms.append(e.dma_start(out=idxp_i[64 * hh:64 * hh + 45, p:p + 1],
                                               in_=idx_d[h:h + 1, 0:45]))
                for m in mms[:-1]:
                    m.then_inc(P.sems["gq"], 16)
                return mms[-1]
            tk_idx_ld = P.op("gq", [tk_idx_st], idx_ld_fn, n_dma=8)
            tk_idx_f = P.op("v", [tk_idx_ld], lambda e:
                            e.tensor_copy(out=idxp_f[:], in_=idxp_i[:]))

        if _MAXP < 4:
            P.op("v", [tk_idx_f], lambda e: e.nop())
            P.finalize(block)
            return nc
        P.barrier()
        p_es = contextlib.ExitStack()
        PSB = lambda n, s, d: p_es.enter_context(nc.sbuf_tensor(n, s, d))
        PPS = lambda n, s, d: p_es.enter_context(nc.psum_tensor(n, s, d))
        with p_es:
            iotaf_s = PSB("iotaf_s", [128, T], F32)
            qsel = PSB("qsel", [128, 512], BF16)
            qselT = PSB("qselT", [128, 128], BF16)
            kp = PSB("kp", [128, T], BF16)
            vp_s = PSB("vp_s", [128, T], BF16)
            mneg = PSB("mneg", [128, T], F32)
            sc = PSB("sc", [128, T], F32)
            attn = PSB("attn", [128, T], BF16)
            attnT = PSB("attnT", [128, T], BF16)
            smax = PSB("smax", [128, 1], F32)
            negmax = PSB("negmax", [128, 1], F32)
            ssum = PSB("ssum", [128, 1], F32)
            rsum = PSB("rsum", [128, 1], F32)
            psS = [PPS(f"psS{i}", [128, 512], F32) for i in range(2)]
            psT = [PPS(f"psT{i}", [128, 128], BF16) for i in range(2)]
            psAV = PPS("psAV", [128, 128], F32)

            ld_iotaf = P.op("sq", [], lambda e: e.dma_start(out=iotaf_s[:], in_=iota_f[:, :]))
            so_done = [None] * 4
            prev_end = None
            prev_qsT = None
            for p in range(4):
                ld_kp = P.op("sq", [k_st[NT1 - 1], prev_end], lambda e, p=p:
                             e.dma_start(out=kp[:], in_=k_d[p * 128:(p + 1) * 128, :]))
                ld_vp = P.op("gq", [qv_st[NT1 - 1], prev_end], lambda e, p=p: e.dma_start(
                    out=vp_s[:].rearrange("p2 (tc d) -> p2 tc d", d=128),
                    in_=v_d[:, p * 128:(p + 1) * 128].rearrange("(tc p2) d -> p2 tc d",
                                                                p2=128)))
                qz = P.op("v", [prev_qsT], lambda e: e.memset(qsel[:], 0.0))
                if _os.environ.get("MK_NO_GATHER"):
                    gth = P.op("gq", [tk_idx_f, qv_st[NT1 - 1], prev_end, qz],
                               lambda e, p=p: e.dma_start(out=qsel[:], in_=q_d[0:128, :]))
                else:
                    gth = P.op("gq", [tk_idx_f, qv_st[NT1 - 1], prev_end, qz],
                               lambda e, p=p: e.indirect_dma_start(
                                   qsel[:], None, q_d[:, :],
                                   bass.IndirectOffsetOnAxis(idxp_i[:, p:p + 1], 0),
                                   bounds_check=T - 1, oob_is_err=False))
                mneg_op = P.op("v", [tk_idx_f, ld_iotaf, prev_end], lambda e, p=p:
                               e.tensor_scalar(mneg[:], iotaf_s[:], idxp_f[:, p:p + 1], -1e30,
                                               op0=ALU.is_gt, op1=ALU.mult))
                qsT = P.op("t", [gth, ld_identb, prev_end], lambda e, p=p:
                           e.transpose(psT[0][:, :], qsel[:, p * 128:(p + 1) * 128],
                                       identb_s[:]))
                prev_qsT = qsT
                qsT_ev = P.op("v", [qsT], lambda e:
                              e.tensor_copy(out=qselT[:], in_=psT[0][:, :]))
                sc_mm = [None] * 16
                sc_ev = [None] * 16
                for ts in range(16):
                    deps = [qsT_ev, ld_kp]
                    if ts >= 2:
                        deps.append(sc_ev[ts - 2])

                    def mk_sc(ts=ts):
                        def f(e):
                            e.matmul(psS[ts % 2][0:64, :], qselT[0:64, 0:64],
                                     kp[0:64, ts * 512:(ts + 1) * 512], start=True, stop=True)
                            return e.matmul(psS[ts % 2][64:128, :], qselT[64:128, 64:128],
                                            kp[64:128, ts * 512:(ts + 1) * 512],
                                            start=True, stop=True)
                        return f
                    sc_mm[ts] = P.op("t", deps, mk_sc())
                    sc_ev[ts] = P.op("v", [sc_mm[ts], mneg_op], lambda e, ts=ts:
                                     e.tensor_add(sc[:, ts * 512:(ts + 1) * 512],
                                                  psS[ts % 2][:, :],
                                                  mneg[:, ts * 512:(ts + 1) * 512]))
                vmax = P.op("v", [sc_ev[15]], lambda e: (
                    e.tensor_reduce(out=smax[:], in_=sc[:], axis=AX.X, op=ALU.max),
                    e.drain(),
                    e.tensor_scalar(negmax[:], smax[:], -0.125, None, op0=ALU.mult))[-1])
                sexp = P.op("s", [vmax], lambda e:
                            e.activation(out=attn[:], in_=sc[:], func=AF.Exp, scale=0.125,
                                         bias=negmax[:], accum_out=ssum[:]))
                at_tr = [None] * 64
                at_ev = [None] * 64
                for tc in range(64):
                    deps = [sexp]
                    if tc >= 2:
                        deps.append(at_ev[tc - 2])
                    at_tr[tc] = P.op("t", deps, lambda e, tc=tc:
                                     e.transpose(psT[tc % 2][:, :],
                                                 attn[:, tc * 128:(tc + 1) * 128], identb_s[:]))
                    at_ev[tc] = P.op("v", [at_tr[tc]], lambda e, tc=tc:
                                     e.tensor_copy(out=attnT[:, tc * 128:(tc + 1) * 128],
                                                   in_=psT[tc % 2][:, :]))

                def mk_av():
                    def f(e):
                        for tc in range(64):
                            mm = e.matmul(psAV[:, :], attnT[:, tc * 128:(tc + 1) * 128],
                                          vp_s[:, tc * 128:(tc + 1) * 128],
                                          start=(tc == 0), stop=(tc == 63))
                        return mm
                    return f
                av = P.op("t", [at_ev[63], ld_vp], mk_av())

                def mk_soev(p=p):
                    def f(e):
                        e.reciprocal(out=rsum[:], in_=ssum[:])
                        e.drain()
                        e.tensor_scalar(so[p][0:64, :], psAV[0:64, 0:64], rsum[0:64, :],
                                        None, op0=ALU.mult)
                        return e.tensor_scalar(so[p][64:128, :], psAV[64:128, 64:128],
                                               rsum[64:128, :], None, op0=ALU.mult)
                    return f
                so_done[p] = P.op("v", [av, sexp], mk_soev())
                prev_end = so_done[p]

        if _MAXP < 5:
            P.op("v", so_done, lambda e: e.nop())
            P.finalize(block)
            return nc
        def ag_store_fn(e):
            for p in range(4):
                e.dma_start(out=ag_in[:, p * 8192:(p + 1) * 8192]
                            .rearrange("o (p2 d) -> (o p2) d", d=64), in_=so[p][:]
                            ).then_inc(P.sems["gq"], 16)
            return e.dma_start(
                out=ag_in[:, 32768:33280].rearrange("o (p2 f) -> (o p2) f", f=4),
                in_=idxp_f[:])
        ag_st = P.op("gq", so_done + [tk_idx_f], ag_store_fn, n_dma=5)
        if _os.environ.get("MK_NO_CC"):
            ag_cc = P.op("g", [ag_st], lambda e: e.engine_nop())
        else:
            ag_cc = P.op("g", [ag_st], lambda e: e.collective_compute(
                "AllGather", ALU.bypass, replica_groups=[[0, 1], [2, 3], [4, 5], [6, 7]],
                ins=[ag_in[:, :]], outs=[ag_out[:, :]]))

        if _MAXP < 6:
            P.op("v", [ag_cc], lambda e: e.nop())
            P.finalize(block)
            return nc
        P.barrier()
        p_es = contextlib.ExitStack()
        PSB = lambda n, s, d: p_es.enter_context(nc.sbuf_tensor(n, s, d))
        PPS = lambda n, s, d: p_es.enter_context(nc.psum_tensor(n, s, d))
        with p_es:
            iotal_s = PSB("iotal_s", [128, TH], F32)
            wo_s = PSB("wo_s", [64, 16 * 1024], BF16)
            sog = PSB("sog", [128, 64], F32)
            soT = PSB("soT", [64, 128], BF16)
            rg = [PSB(f"rg{g}", [128, D], BF16) for g in range(8)]
            pt = [PSB(f"pt{g}", [128, TH], BF16) for g in range(8)]
            idxg_f = PSB("idxg_f", [128, 8], F32)
            xab2 = [PSB(f"xh_{i}", [128, D], F32) for i in range(2)]
            x1b2 = [PSB(f"x1_{i}", [128, D], F32) for i in range(2)]
            xl2b = [PSB(f"xl2_{i}", [128, D], F32) for i in range(2)]
            xqo = [PSB(f"xqo_{i}", [128, D], BF16) for i in range(2)]
            xqt = PSB("xqt", [128, D], F32)
            xqi = PSB("xqi", [128, D], I32d)
            junk2 = PSB("junk2", [128, D], F32)
            mxv = PSB("mxv", [128, 1], F32)
            psTf2 = PPS("psTf2", [128, 128], F32)
            psR = PPS("psR", [128, 512], F32)
            psX = [PPS(f"psX{i}", [128, 512], F32) for i in range(4)]

            ld_iotal = P.op("sq", [], lambda e: e.dma_start(out=iotal_s[:], in_=iota_loc[:, :]))
            ld_wo = []
            for gh in range(16):
                ld_wo.append(P.op("gq", [], lambda e, gh=gh:
                                  e.dma_start(out=wo_s[:, gh * 1024:(gh + 1) * 1024],
                                              in_=wo_i[gh * 64:(gh + 1) * 64, :])))
            idxg_ms = P.op("v", [], lambda e: e.memset(idxg_f[:], 10000000.0))

            def idxg_ld_fn(e):
                mms = []
                for g in range(8):
                    blk, pp = g // 4, g % 4
                    for hh in range(2):
                        base = 32768 + (64 * hh) * 4 + pp
                        src = ag_out[blk:blk + 1, base:base + 45 * 4].rearrange(
                            "o (r f) -> (o r) f", f=4)[:, 0:1]
                        mms.append(e.dma_start(out=idxg_f[64 * hh:64 * hh + 45, g:g + 1],
                                               in_=src, allow_slow_non_contiguous=True))
                for m in mms[:-1]:
                    m.then_inc(P.sems["gq"], 16)
                return mms[-1]
            idxg_ld = P.op("gq", [ag_cc, idxg_ms], idxg_ld_fn, n_dma=16)

            pt_ops = []
            for g in range(8):
                pt_ops.append(P.op("v", [idxg_ld, ld_iotal], lambda e, g=g:
                                   e.tensor_scalar(pt[g][:], iotal_s[:], idxg_f[:, g:g + 1],
                                                   None, op0=ALU.is_equal)))

            rg_ops = [None] * 8
            prev_sog = None
            for g in range(8):
                blk, pp = g // 4, g % 4
                ld_sog = P.op("gq", [ag_cc, prev_sog], lambda e, blk=blk, pp=pp:
                              e.dma_start(out=sog[:],
                                          in_=ag_out[blk:blk + 1, pp * 8192:(pp + 1) * 8192]
                                          .rearrange("o (p2 d) -> (o p2) d", d=64)))
                trT = P.op("t", [ld_sog, ld_identf, prev_sog], lambda e:
                           e.transpose(psTf2[0:64, :], sog[:], identf_s[:]))
                trEv = P.op("v", [trT], lambda e:
                            e.tensor_copy(out=soT[:], in_=psTf2[0:64, :]))
                rg_ev = [None, None]
                for ns in range(2):
                    deps_mm = [trEv] + ([ld_wo[-1]] if ns == 0 else [rg_ev[0]])

                    def mk_rmm(g=g, ns=ns):
                        def f(e):
                            mm = None
                            for hh in range(2):
                                gh = 2 * g + hh
                                mm = e.matmul(psR[64 * hh:64 * hh + 64, :],
                                              soT[:, 64 * hh:64 * hh + 64],
                                              wo_s[:, gh * 1024 + ns * 512:
                                                   gh * 1024 + (ns + 1) * 512],
                                              start=True, stop=True)
                            return mm
                        return f
                    rmm = P.op("t", deps_mm, mk_rmm())
                    rg_ev[ns] = P.op("v", [rmm], lambda e, g=g, ns=ns:
                                     e.tensor_copy(out=rg[g][:, ns * 512:(ns + 1) * 512],
                                                   in_=psR[:, :]))
                rg_ops[g] = rg_ev[1]
                prev_sog = rg_ev[1]

            cmx_ops = [None] * NT2
            x1_st = [None] * NT2
            xl2_st = [None] * NT2
            ln2_st1 = [None] * NT2
            for tb in range(NT2):
                bufx, bufx1, bufl2 = xab2[tb % 2], x1b2[tb % 2], xl2b[tb % 2]
                ldx = P.op("sq", [cmx_ops[tb - 2] if tb >= 2 else None],
                           lambda e, tb=tb, bufx=bufx:
                           e.dma_start(out=bufx[:], in_=x_half[tb * 128:(tb + 1) * 128, :]))
                mm_deps = [pt_ops[-1], rg_ops[-1], ld_bo, ld_ones]
                if tb >= 2:
                    mm_deps.append(x1_st[tb - 2])

                def mk_sct(tb=tb):
                    def f(e):
                        mm = None
                        for ns in range(2):
                            bank = (tb % 2) * 2 + ns
                            for g in range(8):
                                e.matmul(psX[bank][:, :], pt[g][:, tb * 128:(tb + 1) * 128],
                                         rg[g][:, ns * 512:(ns + 1) * 512],
                                         start=(g == 0), stop=False)
                            mm = e.matmul(psX[bank][:, :], onesb_s[:, :],
                                          bo_s[:, ns * 512:(ns + 1) * 512],
                                          start=False, stop=True)
                        return mm
                    return f
                sct = P.op("t", mm_deps, mk_sct())

                def mk_x1(tb=tb, bufx=bufx, bufx1=bufx1):
                    def f(e):
                        for ns in range(2):
                            mm = e.tensor_add(bufx1[:, ns * 512:(ns + 1) * 512],
                                              psX[(tb % 2) * 2 + ns][:, :],
                                              bufx[:, ns * 512:(ns + 1) * 512])
                        return mm
                    return f
                x1v = P.op("v", [sct, ldx], mk_x1())
                x1_st[tb] = P.op("gq", [x1v], lambda e, tb=tb, bufx1=bufx1:
                                 e.dma_start(out=x1_d[tb * 128:(tb + 1) * 128, :],
                                             in_=bufx1[:]))
                sq1 = P.op("s", [x1v, ln2_st1[tb - 1] if tb >= 1 else None],
                           lambda e, bufx1=bufx1:
                           e.activation(out=junk2[:], in_=bufx1[:], func=AF.Square,
                                        accum_out=lnt[:, 1:2]))
                st1 = P.op("v", [x1v, sq1], lambda e, bufx1=bufx1: (
                    e.tensor_reduce(out=lnt[:, 0:1], in_=bufx1[:], axis=AX.X, op=ALU.add),
                    e.drain(),
                    e.tensor_scalar(lnt[:, 2:3], lnt[:, 0:1], 1.0 / D, None, op0=ALU.mult),
                    e.drain(),
                    e.scalar_tensor_tensor(out=lnt[:, 5:6], in0=lnt[:, 2:3],
                                           scalar=lnt[:, 2:3], in1=eps_s[:],
                                           op0=ALU.mult, op1=ALU.subtract),
                    e.drain(),
                    e.tensor_scalar(lnt[:, 3:4], lnt[:, 1:2], 1.0 / D, lnt[:, 5:6],
                                    op0=ALU.mult, op1=ALU.subtract))[-1])
                ln2_st1[tb] = st1
                sq2 = P.op("s", [st1], lambda e:
                           e.activation(out=lnt[:, 4:5], in_=lnt[:, 3:4], func=AF.Sqrt))
                fin = P.op("v", [sq2, x1_st[tb]], lambda e, bufx1=bufx1, bufl2=bufl2: (
                    e.reciprocal(out=lnt[:, 6:7], in_=lnt[:, 4:5]),
                    e.drain(),
                    e.tensor_scalar(lnt[:, 7:8], lnt[:, 2:3], lnt[:, 6:7], -1.0,
                                    op0=ALU.mult, op1=ALU.mult),
                    e.drain(),
                    e.scalar_tensor_tensor(out=bufl2[:], in0=bufx1[:], scalar=lnt[:, 6:7],
                                           in1=lnt[:, 7:8].to_broadcast([128, D]),
                                           op0=ALU.mult, op1=ALU.add))[-1])
                cmx_ops[tb] = P.op("v", [fin], lambda e, tb=tb, bufl2=bufl2:
                                   e.tensor_reduce(out=cmx[:, tb:tb + 1], in_=bufl2[:],
                                                   axis=AX.X, op=ALU.max,
                                                   apply_absolute_value=True))
                xl2_st[tb] = P.op("gq", [cmx_ops[tb]], lambda e, tb=tb, bufl2=bufl2:
                                  e.dma_start(out=xln2_d[tb * 128:(tb + 1) * 128, :],
                                              in_=bufl2[:]))

            vmx = P.op("v", [cmx_ops[-1]], lambda e: (
                e.tensor_reduce(out=mxv[:], in_=cmx[:, 0:NT2], axis=AX.X, op=ALU.max),
                e.drain(), e.nop())[-1])
            tmx = P.op("t", [vmx, ld_identf], lambda e:
                       e.transpose(psTf2[0:1, :], mxv[:], identf_s[:]))
            vmx2 = P.op("v", [tmx], lambda e: (
                e.tensor_reduce(out=mxv[0:1, 0:1], in_=psTf2[0:1, :], axis=AX.X, op=ALU.max),
                e.drain(),
                e.tensor_copy(out=mxrow[:], in_=mxv[0:1, 0:1].to_broadcast([1, 2048])))[-1])
            mx1_st = P.op("gq", [vmx2], lambda e: e.dma_start(out=mx1_in[:, :], in_=mxrow[:]))
            if _os.environ.get("MK_NO_CC"):
                mx1_cc = P.op("g", [mx1_st], lambda e: e.engine_nop())
            else:
                mx1_cc = P.op("g", [mx1_st], lambda e: e.collective_compute(
                    "AllReduce", ALU.max, replica_groups=[[0, 1, 2, 3, 4, 5, 6, 7]],
                    ins=[mx1_in[:, :]], outs=[mx1_out[:, :]]))
            mx1_ld = P.op("gq", [mx1_cc], lambda e: e.dma_start(
                out=mxg[:, 0:16], in_=mx1_out[:, :].rearrange("o (p2 f) -> (o p2) f", f=16)))
            sigx = P.op("s", [ld_clip], lambda e:
                        e.activation(out=scr[:, 0:1], in_=clip_s[:, 0:1], func=AF.Sigmoid))
            scx = P.op("v", [sigx, mx1_ld, wq_v[-1]], lambda e: (
                e.tensor_tensor(scr[:, 1:2], scr[:, 0:1], mxg[:, 0:1], op=ALU.mult),
                e.drain(),
                e.tensor_scalar(scr[:, 2:3], scr[:, 1:2], 1.0 / 127.0, None, op0=ALU.mult),
                e.drain(),
                e.reciprocal(out=rsx[:], in_=scr[:, 2:3]),
                e.tensor_scalar(sxsg[:, :], sgw[:, :], scr[:, 2:3], None, op0=ALU.mult),
                e.tensor_scalar(sxsu[:, :], suw[:, :], scr[:, 2:3], None, op0=ALU.mult),
                e.drain(),
                e.nop())[-1])

            xq_st = [None] * NT2
            for tb in range(NT2):
                bufl2 = xl2b[tb % 2]
                deps = [xl2_st[tb], scx]
                if tb >= 2:
                    deps.append(xq_st[tb - 2])
                ldl2 = P.op("sq", deps, lambda e, tb=tb, bufl2=bufl2:
                            e.dma_start(out=bufl2[:], in_=xln2_d[tb * 128:(tb + 1) * 128, :]))

                def mk_xq(tb=tb, bufl2=bufl2):
                    def f(e):
                        e.tensor_scalar(xqt[:], bufl2[:], rsx[:], CHI,
                                        op0=ALU.mult, op1=ALU.min)
                        e.drain()
                        e.tensor_scalar(xqt[:], xqt[:], CLO, None, op0=ALU.max)
                        e.drain()
                        e.tensor_copy(out=xqi[:], in_=xqt[:])
                        e.drain()
                        return e.tensor_copy(out=xqo[tb % 2][:], in_=xqi[:])
                    return f
                xqv = P.op("v", [ldl2], mk_xq())
                xq_st[tb] = P.op("gq", [xqv], lambda e, tb=tb:
                                 e.dma_start(out=xq_d[tb * 128:(tb + 1) * 128, :],
                                             in_=xqo[tb % 2][:]))

        if _MAXP < 8:
            P.op("v", [xq_st[-1], scx], lambda e: e.nop())
            P.finalize(block)
            return nc
        P.barrier()
        p_es = contextlib.ExitStack()
        PSB = lambda n, s, d: p_es.enter_context(nc.sbuf_tensor(n, s, d))
        PPS = lambda n, s, d: p_es.enter_context(nc.psum_tensor(n, s, d))
        with p_es:
            xqfm = [PSB(f"xqfm{i}", [128, TH], BF16) for i in range(8)]
            wgT = [PSB(f"wgT{i}", [128, II], BF16) for i in range(8)]
            wuT = [PSB(f"wuT{i}", [128, II], BF16) for i in range(8)]
            sgbuf = [PSB(f"sgbuf{i}", [128, 512], F32) for i in range(2)]
            stile = [PSB(f"stile{i}", [128, 512], BF16) for i in range(2)]
            smxc = PSB("smxc", [128, 176], F32)
            mxv2 = PSB("mxv2", [128, 1], F32)
            psG = [PPS(f"psG{i}", [128, 512], F32) for i in range(4)]
            psU = [PPS(f"psU{i}", [128, 512], F32) for i in range(4)]

            xqfm_ld = [P.op("aq", [xq_st[-1]], lambda e, st=st:
                            e.dma_start(out=xqfm[st][:], in_=xq_d[:, st * 128:(st + 1) * 128],
                                        transpose=True)) for st in range(8)]
            wgT_ld = [P.op("aq", [wg_stored], lambda e, st=st:
                           e.dma_start(out=wgT[st][:], in_=wgq_d[:, st * 128:(st + 1) * 128],
                                       transpose=True)) for st in range(8)]
            wuT_ld = [P.op("aq", [wu_stored], lambda e, st=st:
                           e.dma_start(out=wuT[st][:], in_=wuq_d[:, st * 128:(st + 1) * 128],
                                       transpose=True)) for st in range(8)]

            smx_last = None
            s_st_last = None
            prev_sil = [None, None]
            prev_sv = {}
            prev_item_store = {}
            n_item = 0
            for ic in range(NIC):
                for half in range(2):
                    mm_deps = [xqfm_ld[-1], wgT_ld[-1], wuT_ld[-1]]
                    if n_item >= 1:
                        mm_deps.append(prev_item_store[n_item - 1])

                    def mk_gu(ic=ic, half=half):
                        def f(e):
                            for hc in range(8):
                                for ts in range(4):
                                    tsg = half * 4 + ts
                                    e.matmul(psG[ts][:, :],
                                             wgT[hc][:, ic * 128:(ic + 1) * 128],
                                             xqfm[hc][:, tsg * 512:(tsg + 1) * 512],
                                             start=(hc == 0), stop=(hc == 7))
                                for ts in range(4):
                                    tsg = half * 4 + ts
                                    mm = e.matmul(psU[ts][:, :],
                                                  wuT[hc][:, ic * 128:(ic + 1) * 128],
                                                  xqfm[hc][:, tsg * 512:(tsg + 1) * 512],
                                                  start=(hc == 0), stop=(hc == 7))
                            return mm
                        return f
                    gu = P.op("t", mm_deps, mk_gu())
                    last_store = None
                    for ts in range(4):
                        tsg = half * 4 + ts
                        par = (n_item * 4 + ts) % 2
                        sil = P.op("s", [gu, scx,
                                         prev_sv.get((n_item * 4 + ts) - 2)],
                                   lambda e, ic=ic, ts=ts, par=par:
                                   e.activation(out=sgbuf[par][:], in_=psG[ts][:, :],
                                                func=AF.Silu, scale=sxsg[:, ic:ic + 1]))
                        sv = P.op("v", [sil], lambda e, ic=ic, ts=ts, par=par:
                                  e.scalar_tensor_tensor(out=stile[par][:],
                                                         in0=psU[ts][:, :],
                                                         scalar=sxsu[:, ic:ic + 1],
                                                         in1=sgbuf[par][:],
                                                         op0=ALU.mult, op1=ALU.mult))
                        smx = P.op("v", [sv], lambda e, n=n_item, ts=ts, par=par:
                                   e.tensor_reduce(out=smxc[:, (n * 4 + ts) % 176:
                                                            (n * 4 + ts) % 176 + 1],
                                                   in_=stile[par][:], axis=AX.X,
                                                   op=ALU.max, apply_absolute_value=True))
                        sst = P.op("gq", [smx], lambda e, ic=ic, tsg=tsg, par=par:
                                   e.dma_start(out=s_d[ic * 128:(ic + 1) * 128,
                                                       tsg * 512:(tsg + 1) * 512],
                                               in_=stile[par][:]))
                        prev_sv[n_item * 4 + ts] = sst
                        last_store = sst
                        smx_last = smx
                    prev_item_store[n_item] = last_store
                    n_item += 1
            s_st_last = last_store

            vmxs = P.op("v", [smx_last], lambda e: (
                e.tensor_reduce(out=mxv2[:], in_=smxc[:, :], axis=AX.X, op=ALU.max),
                e.drain(), e.nop())[-1])
            tmxs = P.op("t", [vmxs, ld_identf], lambda e:
                        e.transpose(psG[0][0:1, 0:128], mxv2[:], identf_s[:]))
            vmxs2 = P.op("v", [tmxs], lambda e: (
                e.tensor_reduce(out=mxv2[0:1, 0:1], in_=psG[0][0:1, 0:128], axis=AX.X, op=ALU.max),
                e.drain(),
                e.tensor_copy(out=mxrow[:], in_=mxv2[0:1, 0:1].to_broadcast([1, 2048])))[-1])
            mx2_st = P.op("gq", [vmxs2], lambda e: e.dma_start(out=mx2_in[:, :], in_=mxrow[:]))
            if _os.environ.get("MK_NO_CC"):
                mx2_cc = P.op("g", [mx2_st], lambda e: e.engine_nop())
            else:
                mx2_cc = P.op("g", [mx2_st], lambda e: e.collective_compute(
                    "AllReduce", ALU.max, replica_groups=[[0, 1, 2, 3, 4, 5, 6, 7]],
                    ins=[mx2_in[:, :]], outs=[mx2_out[:, :]]))
            mx2_ld = P.op("gq", [mx2_cc], lambda e: e.dma_start(
                out=mxg[:, 0:16], in_=mx2_out[:, :].rearrange("o (p2 f) -> (o p2) f", f=16)))
            sigh = P.op("s", [mx2_ld, ld_clip], lambda e:
                        e.activation(out=scr[:, 0:1], in_=clip_s[:, 1:2], func=AF.Sigmoid))
            sch = P.op("v", [sigh, mx2_ld], lambda e: (
                e.tensor_tensor(scr[:, 1:2], scr[:, 0:1], mxg[:, 0:1], op=ALU.mult),
                e.drain(),
                e.tensor_scalar(shv[:], scr[:, 1:2], 1.0 / 127.0, None, op0=ALU.mult),
                e.drain(),
                e.reciprocal(out=rsh[:], in_=shv[:]),
                e.drain(),
                e.nop())[-1])

        if _MAXP < 9:
            P.op("v", [sch, s_st_last], lambda e: e.nop())
            P.finalize(block)
            return nc
        P.barrier()
        p_es = contextlib.ExitStack()
        PSB = lambda n, s, d: p_es.enter_context(nc.sbuf_tensor(n, s, d))
        PPS = lambda n, s, d: p_es.enter_context(nc.psum_tensor(n, s, d))
        with p_es:
            wdT = [PSB(f"wdT{i}", [128, D], BF16) for i in range(NIC)]
            hq = [PSB(f"hq{i}", [128, 512], BF16) for i in range(NIC)]
            sld = [PSB(f"sld{i}", [128, 512], BF16) for i in range(NIC)]
            qtmp = PSB("qtmp", [128, 512], F32)
            qtmpi = PSB("qtmpi", [128, 512], I32d)
            x1r = [PSB(f"x1r_{i}", [128, D], F32) for i in range(2)]
            outb = [PSB(f"out_{i}", [128, D], F32) for i in range(2)]
            psD = [PPS(f"psD{i}", [128, 512], F32) for i in range(8)]

            if "hq_dbg" in debug:
                P.op("gq", [sch], lambda e: (
                    e.dma_start(out=sh_dbg[:, 0:1], in_=shv[:], allow_slow_non_contiguous=True).then_inc(P.sems["gq"], 16),
                    e.dma_start(out=sh_dbg[:, 1:2], in_=rsh[:], allow_slow_non_contiguous=True))[-1], n_dma=2)
            wdT_ld = [P.op("aq", [wd_stored], lambda e, ic=ic:
                           e.dma_start(out=wdT[ic][:], in_=wdq_d[:, ic * 128:(ic + 1) * 128],
                                       transpose=True)) for ic in range(NIC)]
            if "hq_dbg" in debug:
                for ic in range(NIC):
                    P.op("gq", [wdT_ld[ic]], lambda e, ic=ic:
                         e.dma_start(out=wdt_dbg[ic * 128:(ic + 1) * 128, :], in_=wdT[ic][:]))
            out_st = []
            prev_hq_last = None
            prev_mm_last = None
            prev_ev = {}
            for ts in range(8):
                hq_ops = []
                for ic in range(NIC):
                    ld_s = P.op("sq", [s_st_last if ts == 0 else None,
                                       prev_hq_last if ts >= 1 else None],
                                lambda e, ic=ic, ts=ts:
                                e.dma_start(out=sld[ic][:],
                                            in_=s_d[ic * 128:(ic + 1) * 128,
                                                    ts * 512:(ts + 1) * 512]))
                    deps_h = [ld_s, sch]
                    if ts >= 1:
                        deps_h.append(prev_mm_last)

                    def mk_hq(ic=ic):
                        def f(e):
                            e.tensor_scalar(qtmp[:], sld[ic][:], rsh[:], CHI,
                                            op0=ALU.mult, op1=ALU.min)
                            e.drain()
                            e.tensor_scalar(qtmp[:], qtmp[:], CLO, None, op0=ALU.max)
                            e.drain()
                            e.tensor_copy(out=qtmpi[:], in_=qtmp[:])
                            e.drain()
                            return e.tensor_copy(out=hq[ic][:], in_=qtmpi[:])
                        return f
                    hq_ops.append(P.op("v", deps_h, mk_hq()))
                    if ts == 0 and "hq_dbg" in debug:
                        P.op("gq", [hq_ops[-1]], lambda e, ic=ic:
                             e.dma_start(out=hq_dbg[ic * 128:(ic + 1) * 128, :],
                                         in_=hq[ic][:]))
                prev_hq_last = hq_ops[-1]
                mm_last = None
                for tb in range(4):
                    dep_x1 = prev_ev.get((ts, tb - 2)) if tb >= 2 else \
                        prev_ev.get((ts - 1, tb + 2))
                    ldx1 = P.op("sq", [dep_x1],
                                lambda e, ts=ts, tb=tb:
                                e.dma_start(out=x1r[tb % 2][:],
                                            in_=x1_d[ts * 512 + tb * 128:
                                                     ts * 512 + (tb + 1) * 128, :]))
                    for hn in range(2):
                        deps_mm = [hq_ops[-1], wdT_ld[-1]]
                        if ts >= 1:
                            deps_mm.append(prev_ev.get((ts - 1, tb)))

                        def mk_dn(tb=tb, hn=hn):
                            def f(e):
                                for ic in range(NIC):
                                    mm = e.matmul(psD[tb * 2 + hn][:, :],
                                                  hq[ic][:, tb * 128:(tb + 1) * 128],
                                                  wdT[ic][:, hn * 512:(hn + 1) * 512],
                                                  start=(ic == 0), stop=(ic == NIC - 1))
                                return mm
                            return f
                        mm_last = P.op("t", deps_mm, mk_dn())
                        if hn == 0:
                            mm0 = mm_last

                    def mk_ev(ts=ts, tb=tb):
                        def f(e):
                            for hn in range(2):
                                mm = e.scalar_tensor_tensor(
                                    out=outb[tb % 2][:, hn * 512:(hn + 1) * 512],
                                    in0=psD[tb * 2 + hn][:, :], scalar=shv[:],
                                    in1=x1r[tb % 2][:, hn * 512:(hn + 1) * 512],
                                    op0=ALU.mult, op1=ALU.add)
                            return mm
                        return f
                    ev = P.op("v", [mm_last, ldx1, out_st[-2] if len(out_st) >= 2 else None],
                              mk_ev())
                    prev_ev[(ts, tb)] = ev
                    out_st.append(P.op("gq", [ev], lambda e, ts=ts, tb=tb:
                                       e.dma_start(out=y[ts * 512 + tb * 128:
                                                         ts * 512 + (tb + 1) * 128, :],
                                                   in_=outb[tb % 2][:])))
                prev_mm_last = mm_last

            P.op("v", [out_st[-1]], lambda e: e.nop())

        P.finalize(block)
    return nc


def make_in_maps(inputs):
    x = np.ascontiguousarray(np.asarray(inputs["L0_attn_x"], dtype=np.float32))
    Wq = np.asarray(inputs["L0_attn_Wq"], np.float32)
    Wk = np.asarray(inputs["L0_attn_Wk"], np.float32)
    Wv = np.asarray(inputs["L0_attn_Wv"], np.float32)
    Wo = np.asarray(inputs["L0_attn_Wo"], np.float32)
    bq = np.asarray(inputs["L0_attn_bq"], np.float32)
    bk = np.asarray(inputs["L0_attn_bk"], np.float32)
    bv = np.asarray(inputs["L0_attn_bv"], np.float32)
    bo = np.asarray(inputs["L0_attn_bo"], np.float32)
    clip_x = np.asarray(inputs["L0_ffn_clip_x"], np.float32)
    clip_down = np.asarray(inputs["L0_ffn_clip_down"], np.float32)
    wg = np.asarray(inputs["L0_ffn_w_gate"], np.float32)
    wu = np.asarray(inputs["L0_ffn_w_up"], np.float32)
    wd = np.asarray(inputs["L0_ffn_w_down"], np.float32)

    ident_b = np.eye(128, dtype=ml_dtypes.bfloat16)
    ident_f = np.eye(128, dtype=np.float32)
    ones_b = np.ones((1, 128), dtype=ml_dtypes.bfloat16)
    clip_rep = np.repeat(np.array([[clip_x[0], clip_down[0]]], np.float32), 128, axis=0)
    iota_full = np.repeat(np.arange(T, dtype=np.float32)[None, :], 128, axis=0)
    in_maps = []
    for c in range(NCORES):
        b, hg = c // 2, c % 2
        hs = slice(hg * 512, (hg + 1) * 512)
        off = hg * TH
        iota_l = np.repeat(np.arange(off, off + TH, dtype=np.float32)[None, :], 128, axis=0)
        in_maps.append({
            "x_full": x[b],
            "x_half": np.ascontiguousarray(x[b, off:off + TH]),
            "wq": np.ascontiguousarray(Wq[:, hs]),
            "wk": np.ascontiguousarray(Wk[:, hs]),
            "wv": np.ascontiguousarray(Wv[:, hs]),
            "wo": Wo,
            "bq_r": np.repeat(bq[None, hs], 128, axis=0),
            "bv_r": np.repeat(bv[None, hs], 128, axis=0),
            "bk_c": np.ascontiguousarray(bk[hs].reshape(4, 128).T),
            "bo_b": bo[None, :],
            "wg": wg, "wu": wu, "wd": wd,
            "clip_r": clip_rep,
            "iota_f": iota_full,
            "iota_loc": iota_l,
            "ident_b": ident_b,
            "ident_f": ident_f,
            "ones_b": ones_b,
        })
    return in_maps


def kernel(**inputs):
    in_maps = make_in_maps(inputs)
    nc = build_nc()
    res = run_bass_kernel_spmd(nc, in_maps, core_ids=list(range(NCORES)))
    out = np.empty((B, T, D), np.float32)
    for c in range(NCORES):
        b, th = c // 2, c % 2
        out[b, th * TH:(th + 1) * TH] = res.results[c]["y"]
    return out


# version 09 (final)
# speedup vs baseline: 1.0805x; 1.0805x over previous
import contextlib
import os as _os
import numpy as np
import ml_dtypes
import concourse.bass as bass
import concourse.mybir as mybir
from concourse.bass_utils import run_bass_kernel_spmd

F32 = mybir.dt.float32
BF16 = mybir.dt.bfloat16
U32 = mybir.dt.uint32
I32d = mybir.dt.int32
AF = mybir.ActivationFunctionType
ALU = mybir.AluOpType
AX = mybir.AxisListType

B, T, D = 4, 8192, 1024
HL = 8
II = 2816
TH = T // 2
NT1 = T // 128
NT2 = TH // 128
NIC = II // 128
R23 = float(2 ** 23)
CHI = 127.499
CLO = -128.499
AGB = 36864
NCORES = 8


class Op:
    __slots__ = ("eng", "fn", "deps", "seq", "drain")

    def __init__(self, eng, fn, deps):
        self.eng, self.fn, self.deps = eng, fn, list(deps)
        self.seq = None


class Prog:
    PHYS = {"v": "vector", "s": "scalar", "t": "tensor", "g": "gpsimd",
            "sq": "sync", "aq": "scalar", "gq": "gpsimd"}
    AMT = {"v": 1, "s": 1, "t": 1, "g": 1, "sq": 16, "aq": 16, "gq": 16}

    def __init__(self, nc, es):
        self.nc = nc
        self.ops = []
        self.sems = {k: es.enter_context(nc.semaphore(f"sem_{k}")) for k in self.PHYS}
        self.cum = {k: 0 for k in self.PHYS}
        self.barrier_ops = None
        self.barrier_seen = set()

    def op(self, eng, deps, fn, n_dma=1, drain=True):
        deps = [d for d in deps if d is not None]
        if self.barrier_ops is not None and eng not in self.barrier_seen:
            deps = deps + self.barrier_ops
            self.barrier_seen.add(eng)
        o = Op(eng, fn, deps)
        o.drain = drain
        self.cum[eng] += self.AMT[eng] * (n_dma if eng in ("sq", "aq", "gq") else 1)
        o.seq = self.cum[eng]
        self.ops.append(o)
        return o

    def barrier(self):
        last = {}
        for o in self.ops:
            last[o.eng] = o
        self.barrier_ops = list(last.values())
        self.barrier_seen = set()

    def finalize(self, block):
        streams = {}
        for o in self.ops:
            streams.setdefault(self.PHYS[o.eng], []).append(o)
        for phys in ("sync", "vector", "scalar", "tensor", "gpsimd"):
            ops = streams.get(phys)
            if not ops:
                continue

            def emit(e, ops=ops):
                for o in ops:
                    waited = {}
                    for d in o.deps:
                        if d.eng == o.eng:
                            continue
                        key = d.eng
                        waited[key] = max(waited.get(key, 0), d.seq)
                    for k, v in waited.items():
                        e.wait_ge(self.sems[k], v)
                    last = o.fn(e)
                    if o.eng in ("v", "s") and o.drain:
                        last = e.drain()
                    last.then_inc(self.sems[o.eng], self.AMT[o.eng])
            getattr(block, phys)(emit)


def build_nc(debug=()):
    _MAXP = int(_os.environ.get("MK_MAXPHASE", "9"))
    nc = bass.Bass("TRN2", target_bir_lowering=False)

    def dram(name, shp, dtype):
        k = "ExternalOutput" if name in debug else "Internal"
        return nc.dram_tensor(name, shp, dtype, kind=k).ap()

    din = lambda name, shp, dtype: nc.dram_tensor(name, shp, dtype, kind="ExternalInput").ap()
    x_full = din("x_full", [T, D], F32)
    x_half = din("x_half", [TH, D], F32)
    wq_i = din("wq", [D, 512], F32)
    wk_i = din("wk", [D, 512], F32)
    wv_i = din("wv", [D, 512], F32)
    wo_i = din("wo", [D, D], F32)
    bq_r = din("bq_r", [128, 512], F32)
    bv_r = din("bv_r", [128, 512], F32)
    bk_c = din("bk_c", [128, 4], F32)
    bo_b = din("bo_b", [1, D], F32)
    wg_i = din("wg", [II, D], F32)
    wu_i = din("wu", [II, D], F32)
    wd_i = din("wd", [D, II], F32)
    clip_r = din("clip_r", [128, 2], F32)
    iota_f = din("iota_f", [128, T], F32)
    iota_loc = din("iota_loc", [128, TH], F32)
    ident_b = din("ident_b", [128, 128], BF16)
    ident_f = din("ident_f", [128, 128], F32)
    ones_b = din("ones_b", [1, 128], BF16)
    y = nc.dram_tensor("y", [TH, D], F32, kind="ExternalOutput").ap()
    xln_d = dram("xln_d", [T, D], BF16)
    q_d = dram("q_d", [T, 512], BF16)
    k_d = dram("k_d", [512, T], BF16)
    v_d = dram("v_d", [T, 512], BF16)
    qn_d = dram("qn_d", [HL, T], F32)
    idx_d = dram("idx_d", [HL, 48], U32)
    x1_d = dram("x1_d", [TH, D], F32)
    xln2_d = dram("xln2_d", [TH, D], F32)
    xq_d = dram("xq_d", [TH, D], BF16)
    s_d = dram("s_d", [II, TH], BF16)
    wgq_d = dram("wgq_d", [II, D], BF16)
    wuq_d = dram("wuq_d", [II, D], BF16)
    wdq_d = dram("wdq_d", [D, II], BF16)
    ag_in = dram("ag_in", [1, AGB], F32)
    ag_out = dram("ag_out", [2, AGB], F32)
    mx1_in = dram("mx1_in", [1, 2048], F32)
    mx1_out = dram("mx1_out", [1, 2048], F32)
    mx2_in = dram("mx2_in", [1, 2048], F32)
    mx2_out = dram("mx2_out", [1, 2048], F32)
    hq_dbg = dram("hq_dbg", [II, 512], BF16)
    wdt_dbg = dram("wdt_dbg", [II, D], BF16)
    sh_dbg = dram("sh_dbg", [128, 2], F32)

    g_es = contextlib.ExitStack()
    GSB = lambda n, s, d: g_es.enter_context(nc.sbuf_tensor(n, s, d))
    with g_es:
        P = Prog(nc, g_es)
        eps_s = GSB("eps_s", [128, 1], F32)
        lnt = GSB("lnt", [128, 8], F32)
        identb_s = GSB("identb_s", [128, 128], BF16)
        identf_s = GSB("identf_s", [128, 128], F32)
        onesb_s = GSB("onesb_s", [1, 128], BF16)
        clip_s = GSB("clip_s", [128, 2], F32)
        qn_tm = GSB("qn_tm", [128, NT1 * 8], F32)
        tidx = GSB("tidx", [HL, 48], U32)
        idxp_i = GSB("idxp_i", [128, 4], U32)
        idxp_f = GSB("idxp_f", [128, 4], F32)
        sgw = GSB("sgw", [128, NIC], F32)
        suw = GSB("suw", [128, NIC], F32)
        sdw = GSB("sdw", [128, 8], F32)
        rsx = GSB("rsx", [128, 1], F32)
        rsh = GSB("rsh", [128, 1], F32)
        shv = GSB("shv", [128, 1], F32)
        sxsg = GSB("sxsg", [128, NIC], F32)
        sxsu = GSB("sxsu", [128, NIC], F32)
        mxg = GSB("mxg", [128, 16], F32)
        mxrow = GSB("mxrow", [1, 2048], F32)
        cmx = GSB("cmx", [128, NT2], F32)
        bo_s = GSB("bo_s", [1, D], BF16)
        scr = GSB("scr", [128, 4], F32)
        so = [GSB(f"so{p}", [128, 64], F32) for p in range(4)]

        block = g_es.enter_context(nc.Block())

        ld_identb = P.op("sq", [], lambda e: e.dma_start(out=identb_s[:], in_=ident_b[:, :]))
        ld_identf = P.op("sq", [], lambda e: e.dma_start(out=identf_s[:], in_=ident_f[:, :]))
        ld_ones = P.op("sq", [], lambda e: e.dma_start(out=onesb_s[:], in_=ones_b[:, :]))
        ld_clip = P.op("sq", [], lambda e: e.dma_start(out=clip_s[:], in_=clip_r[:, :]))
        ld_bo = P.op("gq", [], lambda e: e.dma_start(out=bo_s[:], in_=bo_b[:, :]))
        op_eps = P.op("v", [], lambda e: e.memset(eps_s[:], 1e-5))

        p_es = contextlib.ExitStack()
        PSB = lambda n, s, d: p_es.enter_context(nc.sbuf_tensor(n, s, d))
        PPS = lambda n, s, d: p_es.enter_context(nc.psum_tensor(n, s, d))
        with p_es:
            xab = [PSB(f"x_{i}", [128, D], F32) for i in range(4)]
            junk = PSB("junk", [128, D], F32)
            lnt2c = [PSB(f"lnt2c{i}", [128, 8], F32) for i in range(4)]
            xlnb = [PSB(f"xln_{i}", [128, D], BF16) for i in range(4)]
            bq_s = PSB("bq_s", [128, 512], F32)
            bv_s = PSB("bv_s", [128, 512], F32)
            bk_s = PSB("bk_s", [128, 4], F32)
            wq_s = PSB("wq_s", [128, 8 * 512], BF16)
            wk_s = PSB("wk_s", [128, 8 * 512], BF16)
            wv_s = PSB("wv_s", [128, 8 * 512], BF16)
            xfm = [PSB(f"xfm{i}", [128, 2 * 1024], BF16) for i in range(8)]
            qt2 = [PSB(f"qt_{i}", [128, 512], BF16) for i in range(2)]
            vt2 = [PSB(f"vt_{i}", [128, 512], BF16) for i in range(2)]
            kt2 = [PSB(f"kt_{i}", [128, 512], BF16) for i in range(2)]
            wqb = [PSB(f"wq_{i}", [128, II], F32) for i in range(2)]
            wtmp = PSB("wtmp", [128, II], F32)
            wtmp2 = PSB("wtmp2", [128, II], I32d)
            wout = PSB("wout", [128, II], BF16)
            wmax = PSB("wmax", [128, 1], F32)
            wrec = PSB("wrec", [128, 1], F32)
            psQ = [PPS(f"psQ{i}", [128, 512], F32) for i in range(2)]
            psV = [PPS(f"psV{i}", [128, 512], F32) for i in range(2)]
            psK = [PPS(f"psK{i}", [128, 512], F32) for i in range(2)]

            ld_b = [P.op("sq", [], lambda e, s=s, d=d: e.dma_start(out=d[:], in_=s[:, :]))
                    for s, d in [(bq_r, bq_s), (bv_r, bv_s), (bk_c, bk_s)]]
            ld_w = []
            for wsrc, wdst in [(wq_i, wq_s), (wk_i, wk_s), (wv_i, wv_s)]:
                for dc in range(8):
                    ld_w.append(P.op("gq", [], lambda e, s=wsrc, d=wdst, dc=dc:
                                     e.dma_start(out=d[:, dc * 512:(dc + 1) * 512],
                                                 in_=s[dc * 128:(dc + 1) * 128, :])))
            lw_q, lw_k, lw_v = ld_w[7], ld_w[15], ld_w[23]

            NG, GT = 8, 1024
            ln_done = [None] * NT1
            ln_store = [None] * NT1
            ln_st1 = [None] * NT1
            qv_mm = [None] * NT1
            qv_ev = [None] * NT1
            qv_st = [None] * NT1
            k_mm = [None] * NT1
            k_ev = [None] * NT1
            k_st = [None] * NT1
            group_last_t = [None] * NG
            xfm_lds = [None] * NG

            ln_sq2 = [None] * NT1

            def emit_ln_front(t):
                buf = xab[t % 4]
                ld = P.op("sq", [ln_done[t - 4] if t >= 4 else None],
                          lambda e, t=t, buf=buf:
                          e.dma_start(out=buf[:], in_=x_full[t * 128:(t + 1) * 128, :]))
                sq1 = P.op("s", [ld, ln_st1[t - 1] if t >= 1 else None],
                           lambda e, buf=buf:
                           e.activation(out=junk[:], in_=buf[:], func=AF.Square,
                                        accum_out=lnt2c[t % 4][:, 0:1]))
                st1 = P.op("v", [ld, op_eps, sq1], lambda e, buf=buf, t=t: (
                    e.tensor_reduce(out=lnt2c[t % 4][:, 1:2], in_=buf[:], axis=AX.X,
                                    op=ALU.add),
                    e.drain(),
                    e.tensor_scalar(lnt2c[t % 4][:, 2:3], lnt2c[t % 4][:, 1:2], 1.0 / D,
                                    None, op0=ALU.mult),
                    e.drain(),
                    e.scalar_tensor_tensor(out=lnt2c[t % 4][:, 5:6],
                                           in0=lnt2c[t % 4][:, 2:3],
                                           scalar=lnt2c[t % 4][:, 2:3], in1=eps_s[:],
                                           op0=ALU.mult, op1=ALU.subtract),
                    e.drain(),
                    e.tensor_scalar(lnt2c[t % 4][:, 3:4], lnt2c[t % 4][:, 0:1], 1.0 / D,
                                    lnt2c[t % 4][:, 5:6],
                                    op0=ALU.mult, op1=ALU.subtract))[-1])
                ln_st1[t] = st1
                ln_sq2[t] = P.op("s", [st1], lambda e, t=t:
                                 e.activation(out=lnt2c[t % 4][:, 4:5],
                                              in_=lnt2c[t % 4][:, 3:4], func=AF.Sqrt))

            def emit_ln_back(t):
                buf, xout = xab[t % 4], xlnb[t % 4]
                ln_done[t] = P.op("v", [ln_sq2[t], ln_store[t - 4] if t >= 4 else None],
                                  lambda e, buf=buf, xout=xout, t=t: (
                    e.reciprocal(out=lnt2c[t % 4][:, 6:7], in_=lnt2c[t % 4][:, 4:5]),
                    e.drain(),
                    e.tensor_scalar(lnt2c[t % 4][:, 7:8], lnt2c[t % 4][:, 2:3],
                                    lnt2c[t % 4][:, 6:7], -1.0,
                                    op0=ALU.mult, op1=ALU.mult),
                    e.drain(),
                    e.scalar_tensor_tensor(out=xout[:], in0=buf[:],
                                           scalar=lnt2c[t % 4][:, 6:7],
                                           in1=lnt2c[t % 4][:, 7:8].to_broadcast([128, D]),
                                           op0=ALU.mult, op1=ALU.add))[-1])
                ln_store[t] = P.op("gq", [ln_done[t]], lambda e, t=t, xout=xout:
                                   e.dma_start(out=xln_d[t * 128:(t + 1) * 128, :],
                                               in_=xout[:]))

            def emit_xfm(g):
                par = g % 2
                lds = []
                for st in range(8):
                    deps = [ln_store[g * 8 + 7]]
                    if g >= 2:
                        deps.append(group_last_t[g - 2])
                    lds.append(P.op("aq", deps, lambda e, g=g, st=st, par=par:
                                    e.dma_start(out=xfm[st][:, par * 1024:(par + 1) * 1024],
                                                in_=xln_d[g * GT:(g + 1) * GT,
                                                          st * 128:(st + 1) * 128],
                                                transpose=True)))
                xfm_lds[g] = lds

            def emit_qkv(g):
                par = g % 2
                for tb in range(8):
                    j = g * 8 + tb
                    deps = [xfm_lds[g][-1], lw_q, lw_v]
                    if j >= 2:
                        deps.append(qv_ev[j - 2])

                    def mk_qv(tb=tb, j=j, par=par):
                        def f(e):
                            sl = slice(par * 1024 + tb * 128, par * 1024 + (tb + 1) * 128)
                            for dc in range(8):
                                e.matmul(psQ[j % 2][:, :], xfm[dc][:, sl],
                                         wq_s[:, dc * 512:(dc + 1) * 512],
                                         start=(dc == 0), stop=(dc == 7))
                            for dc in range(8):
                                mm = e.matmul(psV[j % 2][:, :], xfm[dc][:, sl],
                                              wv_s[:, dc * 512:(dc + 1) * 512],
                                              start=(dc == 0), stop=(dc == 7))
                            return mm
                        return f
                    qv_mm[j] = P.op("t", deps, mk_qv())
                    deps_e = [qv_mm[j], ld_b[0], ld_b[1]]
                    if j >= 2:
                        deps_e.append(qv_st[j - 2])

                    def mk_qvev(j=j):
                        def f(e):
                            e.tensor_add(qt2[j % 2][:], psQ[j % 2][:, :], bq_s[:])
                            e.tensor_add(vt2[j % 2][:], psV[j % 2][:, :], bv_s[:])
                            e.drain()
                            return e.tensor_reduce(
                                out=qn_tm[:, j * 8:(j + 1) * 8],
                                in_=qt2[j % 2][:].rearrange("p (h d) -> p h d", h=8),
                                axis=AX.X, op=ALU.add, apply_absolute_value=True)
                        return f
                    qv_ev[j] = P.op("v", deps_e, mk_qvev())

                    def mk_qvst(j=j, g=g, tb=tb):
                        def f(e):
                            r0 = g * GT + tb * 128
                            e.dma_start(out=q_d[r0:r0 + 128, :], in_=qt2[j % 2][:]
                                        ).then_inc(P.sems["gq"], 16)
                            return e.dma_start(out=v_d[r0:r0 + 128, :], in_=vt2[j % 2][:])
                        return f
                    qv_st[j] = P.op("gq", [qv_ev[j]], mk_qvst(), n_dma=2)
                for kk in range(8):
                    kj = g * 8 + kk
                    mc, ts2 = kk // 2, kk % 2
                    deps = [xfm_lds[g][-1], lw_k]
                    if kj >= 2:
                        deps.append(k_ev[kj - 2])

                    def mk_k(mc=mc, ts2=ts2, kj=kj, par=par):
                        def f(e):
                            sl = slice(par * 1024 + ts2 * 512, par * 1024 + (ts2 + 1) * 512)
                            for dc in range(8):
                                mm = e.matmul(psK[kj % 2][:, :],
                                              wk_s[:, dc * 512 + mc * 128:
                                                   dc * 512 + (mc + 1) * 128],
                                              xfm[dc][:, sl], start=(dc == 0), stop=(dc == 7))
                            return mm
                        return f
                    k_mm[kj] = P.op("t", deps, mk_k())
                    deps_e = [k_mm[kj], ld_b[2]]
                    if kj >= 2:
                        deps_e.append(k_st[kj - 2])
                    k_ev[kj] = P.op("v", deps_e, lambda e, kj=kj, mc=mc:
                                    e.tensor_scalar(kt2[kj % 2][:], psK[kj % 2][:, :],
                                                    bk_s[:, mc:mc + 1], None, op0=ALU.add))
                    k_st[kj] = P.op("gq", [k_ev[kj]], lambda e, kj=kj, g=g, mc=mc, ts2=ts2:
                                    e.dma_start(
                                        out=k_d[mc * 128:(mc + 1) * 128,
                                                g * GT + ts2 * 512:g * GT + ts2 * 512 + 512],
                                        in_=kt2[kj % 2][:]))
                group_last_t[g] = k_mm[g * 8 + 7]

            wq_items = ([("wg", ic) for ic in range(NIC)] +
                        [("wu", ic) for ic in range(NIC)] +
                        [("wd", hc) for hc in range(8)])
            wq_st = [None] * len(wq_items)
            wq_v = [None] * len(wq_items)

            def emit_wq(n):
                kind, ic = wq_items[n]
                buf = wqb[n % 2]
                cols = D if kind != "wd" else II
                src2 = {"wg": wg_i, "wu": wu_i, "wd": wd_i}[kind]
                ld = P.op("sq", [wq_v[n - 2] if n >= 2 else None],
                          lambda e, buf=buf, src2=src2, ic=ic, cols=cols:
                          e.dma_start(out=buf[:, 0:cols],
                                      in_=src2[ic * 128:(ic + 1) * 128, :]))
                deps_v = [ld]
                if n >= 1:
                    deps_v.append(wq_st[n - 1])

                def mk_wqv(kind=kind, ic=ic, cols=cols, buf=buf):
                    def f(e):
                        e.tensor_reduce(out=wmax[:], in_=buf[:, 0:cols], axis=AX.X,
                                        op=ALU.max, apply_absolute_value=True)
                        e.drain()
                        e.reciprocal(out=wrec[:], in_=wmax[:])
                        e.drain()
                        tgt = {"wg": sgw, "wu": suw, "wd": sdw}[kind]
                        e.tensor_scalar(tgt[:, ic:ic + 1], wmax[:], 1.0 / 127.0, None,
                                        op0=ALU.mult)
                        e.tensor_scalar(wtmp[:, 0:cols], buf[:, 0:cols], wrec[:], 127.0,
                                        op0=ALU.mult, op1=ALU.mult)
                        e.drain()
                        e.tensor_copy(out=wtmp2[:, 0:cols], in_=wtmp[:, 0:cols])
                        e.drain()
                        if kind != "wd":
                            return e.tensor_copy(out=wout[:, 0:cols], in_=wtmp2[:, 0:cols])
                        return e.tensor_scalar(wout[:, 0:cols], wtmp2[:, 0:cols],
                                               sdw[:, ic:ic + 1], None, op0=ALU.mult)
                    return f
                wq_v[n] = P.op("v", deps_v, mk_wqv())
                dstm = {"wg": wgq_d, "wu": wuq_d, "wd": wdq_d}[kind]
                wq_st[n] = P.op("gq", [wq_v[n]], lambda e, d=dstm, ic=ic, cols=cols:
                                e.dma_start(out=d[ic * 128:(ic + 1) * 128, :],
                                            in_=wout[:, 0:cols]))

            wq_n = 0
            for g in range(NG):
                for t in range(g * 8, g * 8 + 8):
                    emit_ln_front(t)
                    if t >= 1 and t % 8 != 0:
                        emit_ln_back(t - 1)
                emit_ln_back(g * 8 + 7)
                emit_xfm(g)
                if g >= 1:
                    emit_qkv(g - 1)
                    take = min(8, len(wq_items) - wq_n)
                    for _ in range(take):
                        emit_wq(wq_n)
                        wq_n += 1
            emit_qkv(NG - 1)
            while wq_n < len(wq_items):
                emit_wq(wq_n)
                wq_n += 1
            wg_stored, wu_stored, wd_stored = wq_st[NIC - 1], wq_st[2 * NIC - 1], wq_st[-1]

        if _MAXP < 3:
            P.op("v", [qv_ev[NT1 - 1], wq_v[-1], k_ev[NT1 - 1]], lambda e: e.nop())
            P.finalize(block)
            return nc
        P.barrier()
        p_es = contextlib.ExitStack()
        PSB = lambda n, s, d: p_es.enter_context(nc.sbuf_tensor(n, s, d))
        PPS = lambda n, s, d: p_es.enter_context(nc.psum_tensor(n, s, d))
        with p_es:
            qn_tt = PSB("qn_tt", [128, 512], F32)
            qn_hm = PSB("qn_hm", [HL, T], F32)
            mx8 = PSB("mx8", [HL, 8], F32)
            psTf = PPS("psTf", [128, 128], F32)

            tk_ev = [None] * 4
            tk_st = [None] * 4
            for jb in range(4):
                tr = P.op("t", [qv_ev[NT1 - 1], ld_identf,
                                tk_ev[jb - 1] if jb >= 1 else None],
                          lambda e, jb=jb:
                          e.transpose(psTf[:], qn_tm[:, jb * 128:(jb + 1) * 128], identf_s[:]))
                tk_ev[jb] = P.op("v", [tr], lambda e, jb=jb:
                                 e.tensor_copy(out=qn_tt[:, jb * 128:(jb + 1) * 128],
                                               in_=psTf[:]))
                tk_st[jb] = P.op("gq", [tk_ev[jb]], lambda e, jb=jb: e.dma_start(
                    out=qn_d[:, jb * 2048:(jb + 1) * 2048].rearrange("h (a p) -> a h p", p=128),
                    in_=qn_tt[:, jb * 128:(jb + 1) * 128]))
            tk_ld = P.op("sq", tk_st, lambda e: e.dma_start(out=qn_hm[:], in_=qn_d[:, :]))

            def topk_fn(e):
                for r in range(6):
                    e.max(out=mx8[:], in_=qn_hm[:])
                    e.drain()
                    if r == 5:
                        e.memset(mx8[:, 5:8], -1e30)
                        e.drain()
                    e.max_index(out=tidx[:, r * 8:(r + 1) * 8], in_max=mx8[:],
                                in_values=qn_hm[:])
                    e.drain()
                    e.match_replace(out=qn_hm[:], in_to_replace=mx8[:], in_values=qn_hm[:],
                                    imm_value=-1e30)
                    e.drain()
                e.memset(tidx[:, 45:48], 10000000)
                e.drain()
                return e.memset(idxp_i[:], 10000000)
            tk_main = P.op("v", [tk_ld], topk_fn)
            tk_idx_st = P.op("gq", [tk_main], lambda e: e.dma_start(out=idx_d[:, :], in_=tidx[:]))

            def idx_ld_fn(e):
                mms = []
                for p in range(4):
                    for hh in range(2):
                        h = 2 * p + hh
                        mms.append(e.dma_start(out=idxp_i[64 * hh:64 * hh + 45, p:p + 1],
                                               in_=idx_d[h:h + 1, 0:45]))
                for m in mms[:-1]:
                    m.then_inc(P.sems["gq"], 16)
                return mms[-1]
            tk_idx_ld = P.op("gq", [tk_idx_st], idx_ld_fn, n_dma=8)
            tk_idx_f = P.op("v", [tk_idx_ld], lambda e:
                            e.tensor_copy(out=idxp_f[:], in_=idxp_i[:]))

        if _MAXP < 4:
            P.op("v", [tk_idx_f], lambda e: e.nop())
            P.finalize(block)
            return nc
        P.barrier()
        p_es = contextlib.ExitStack()
        PSB = lambda n, s, d: p_es.enter_context(nc.sbuf_tensor(n, s, d))
        PPS = lambda n, s, d: p_es.enter_context(nc.psum_tensor(n, s, d))
        with p_es:
            iotaf_s = PSB("iotaf_s", [128, T], F32)
            qsel = PSB("qsel", [128, 512], BF16)
            qselT = PSB("qselT", [128, 128], BF16)
            kp = PSB("kp", [128, T], BF16)
            vp_s = PSB("vp_s", [128, T], BF16)
            mneg = PSB("mneg", [128, T], F32)
            sc = PSB("sc", [128, T], F32)
            attn = PSB("attn", [128, T], BF16)
            attnT = PSB("attnT", [128, T], BF16)
            smax = PSB("smax", [128, 1], F32)
            negmax = PSB("negmax", [128, 1], F32)
            ssum = PSB("ssum", [128, 1], F32)
            rsum = PSB("rsum", [128, 1], F32)
            psS = [PPS(f"psS{i}", [128, 512], F32) for i in range(2)]
            psT = [PPS(f"psT{i}", [128, 128], BF16) for i in range(2)]
            psAV = PPS("psAV", [128, 128], F32)

            ld_iotaf = P.op("sq", [], lambda e: e.dma_start(out=iotaf_s[:], in_=iota_f[:, :]))
            so_done = [None] * 4
            prev_end = None
            prev_qsT = None
            for p in range(4):
                ld_kp = P.op("sq", [k_st[NT1 - 1], prev_end], lambda e, p=p:
                             e.dma_start(out=kp[:], in_=k_d[p * 128:(p + 1) * 128, :]))
                ld_vp = P.op("gq", [qv_st[NT1 - 1], prev_end], lambda e, p=p: e.dma_start(
                    out=vp_s[:].rearrange("p2 (tc d) -> p2 tc d", d=128),
                    in_=v_d[:, p * 128:(p + 1) * 128].rearrange("(tc p2) d -> p2 tc d",
                                                                p2=128)))
                qz = P.op("v", [prev_qsT], lambda e: e.memset(qsel[:], 0.0))
                if _os.environ.get("MK_NO_GATHER"):
                    gth = P.op("gq", [tk_idx_f, qv_st[NT1 - 1], prev_end, qz],
                               lambda e, p=p: e.dma_start(out=qsel[:], in_=q_d[0:128, :]))
                else:
                    gth = P.op("gq", [tk_idx_f, qv_st[NT1 - 1], prev_end, qz],
                               lambda e, p=p: e.indirect_dma_start(
                                   qsel[:], None, q_d[:, :],
                                   bass.IndirectOffsetOnAxis(idxp_i[:, p:p + 1], 0),
                                   bounds_check=T - 1, oob_is_err=False))
                mneg_op = P.op("v", [tk_idx_f, ld_iotaf, prev_end], lambda e, p=p:
                               e.tensor_scalar(mneg[:], iotaf_s[:], idxp_f[:, p:p + 1], -1e30,
                                               op0=ALU.is_gt, op1=ALU.mult))
                qsT = P.op("t", [gth, ld_identb, prev_end], lambda e, p=p:
                           e.transpose(psT[0][:, :], qsel[:, p * 128:(p + 1) * 128],
                                       identb_s[:]))
                prev_qsT = qsT
                qsT_ev = P.op("v", [qsT], lambda e:
                              e.tensor_copy(out=qselT[:], in_=psT[0][:, :]))
                sc_mm = [None] * 16
                sc_ev = [None] * 16
                for ts in range(16):
                    deps = [qsT_ev, ld_kp]
                    if ts >= 2:
                        deps.append(sc_ev[ts - 2])

                    def mk_sc(ts=ts):
                        def f(e):
                            e.matmul(psS[ts % 2][0:64, :], qselT[0:64, 0:64],
                                     kp[0:64, ts * 512:(ts + 1) * 512], start=True, stop=True)
                            return e.matmul(psS[ts % 2][64:128, :], qselT[64:128, 64:128],
                                            kp[64:128, ts * 512:(ts + 1) * 512],
                                            start=True, stop=True)
                        return f
                    sc_mm[ts] = P.op("t", deps, mk_sc())
                    sc_ev[ts] = P.op("v", [sc_mm[ts], mneg_op], lambda e, ts=ts:
                                     e.tensor_add(sc[:, ts * 512:(ts + 1) * 512],
                                                  psS[ts % 2][:, :],
                                                  mneg[:, ts * 512:(ts + 1) * 512]))
                vmax = P.op("v", [sc_ev[15]], lambda e: (
                    e.tensor_reduce(out=smax[:], in_=sc[:], axis=AX.X, op=ALU.max),
                    e.drain(),
                    e.tensor_scalar(negmax[:], smax[:], -0.125, None, op0=ALU.mult))[-1])
                sexp = P.op("s", [vmax], lambda e:
                            e.activation(out=attn[:], in_=sc[:], func=AF.Exp, scale=0.125,
                                         bias=negmax[:], accum_out=ssum[:]))
                at_tr = [None] * 64
                at_ev = [None] * 64
                for tc in range(64):
                    deps = [sexp]
                    if tc >= 2:
                        deps.append(at_ev[tc - 2])
                    at_tr[tc] = P.op("t", deps, lambda e, tc=tc:
                                     e.transpose(psT[tc % 2][:, :],
                                                 attn[:, tc * 128:(tc + 1) * 128], identb_s[:]))
                    at_ev[tc] = P.op("v", [at_tr[tc]], lambda e, tc=tc:
                                     e.tensor_copy(out=attnT[:, tc * 128:(tc + 1) * 128],
                                                   in_=psT[tc % 2][:, :]))

                def mk_av():
                    def f(e):
                        for tc in range(64):
                            mm = e.matmul(psAV[:, :], attnT[:, tc * 128:(tc + 1) * 128],
                                          vp_s[:, tc * 128:(tc + 1) * 128],
                                          start=(tc == 0), stop=(tc == 63))
                        return mm
                    return f
                av = P.op("t", [at_ev[63], ld_vp], mk_av())

                def mk_soev(p=p):
                    def f(e):
                        e.reciprocal(out=rsum[:], in_=ssum[:])
                        e.drain()
                        e.tensor_scalar(so[p][0:64, :], psAV[0:64, 0:64], rsum[0:64, :],
                                        None, op0=ALU.mult)
                        return e.tensor_scalar(so[p][64:128, :], psAV[64:128, 64:128],
                                               rsum[64:128, :], None, op0=ALU.mult)
                    return f
                so_done[p] = P.op("v", [av, sexp], mk_soev())
                prev_end = so_done[p]

        if _MAXP < 5:
            P.op("v", so_done, lambda e: e.nop())
            P.finalize(block)
            return nc
        def ag_store_fn(e):
            for p in range(4):
                e.dma_start(out=ag_in[:, p * 8192:(p + 1) * 8192]
                            .rearrange("o (p2 d) -> (o p2) d", d=64), in_=so[p][:]
                            ).then_inc(P.sems["gq"], 16)
            return e.dma_start(
                out=ag_in[:, 32768:33280].rearrange("o (p2 f) -> (o p2) f", f=4),
                in_=idxp_f[:])
        ag_st = P.op("gq", so_done + [tk_idx_f], ag_store_fn, n_dma=5)
        if _os.environ.get("MK_NO_CC"):
            ag_cc = P.op("g", [ag_st], lambda e: e.engine_nop())
        else:
            ag_cc = P.op("g", [ag_st], lambda e: e.collective_compute(
                "AllGather", ALU.bypass, replica_groups=[[0, 1], [2, 3], [4, 5], [6, 7]],
                ins=[ag_in[:, :]], outs=[ag_out[:, :]]))

        if _MAXP < 6:
            P.op("v", [ag_cc], lambda e: e.nop())
            P.finalize(block)
            return nc
        P.barrier()
        p_es = contextlib.ExitStack()
        PSB = lambda n, s, d: p_es.enter_context(nc.sbuf_tensor(n, s, d))
        PPS = lambda n, s, d: p_es.enter_context(nc.psum_tensor(n, s, d))
        with p_es:
            iotal_s = PSB("iotal_s", [128, TH], F32)
            wo_s = PSB("wo_s", [64, 16 * 1024], BF16)
            sog = PSB("sog", [128, 64], F32)
            soT = PSB("soT", [64, 128], BF16)
            rg = [PSB(f"rg{g}", [128, D], BF16) for g in range(8)]
            pt = [PSB(f"pt{g}", [128, TH], BF16) for g in range(8)]
            idxg_f = PSB("idxg_f", [128, 8], F32)
            xab2 = [PSB(f"xh_{i}", [128, D], F32) for i in range(2)]
            x1b2 = [PSB(f"x1_{i}", [128, D], F32) for i in range(2)]
            xl2b = [PSB(f"xl2_{i}", [128, D], F32) for i in range(2)]
            xqo = [PSB(f"xqo_{i}", [128, D], BF16) for i in range(2)]
            xqt = PSB("xqt", [128, D], F32)
            xqi = PSB("xqi", [128, D], I32d)
            junk2 = PSB("junk2", [128, D], F32)
            mxv = PSB("mxv", [128, 1], F32)
            psTf2 = PPS("psTf2", [128, 128], F32)
            psR = PPS("psR", [128, 512], F32)
            psX = [PPS(f"psX{i}", [128, 512], F32) for i in range(4)]

            ld_iotal = P.op("sq", [], lambda e: e.dma_start(out=iotal_s[:], in_=iota_loc[:, :]))
            ld_wo = []
            for gh in range(16):
                ld_wo.append(P.op("gq", [], lambda e, gh=gh:
                                  e.dma_start(out=wo_s[:, gh * 1024:(gh + 1) * 1024],
                                              in_=wo_i[gh * 64:(gh + 1) * 64, :])))
            idxg_ms = P.op("v", [], lambda e: e.memset(idxg_f[:], 10000000.0))

            def idxg_ld_fn(e):
                mms = []
                for g in range(8):
                    blk, pp = g // 4, g % 4
                    for hh in range(2):
                        base = 32768 + (64 * hh) * 4 + pp
                        src = ag_out[blk:blk + 1, base:base + 45 * 4].rearrange(
                            "o (r f) -> (o r) f", f=4)[:, 0:1]
                        mms.append(e.dma_start(out=idxg_f[64 * hh:64 * hh + 45, g:g + 1],
                                               in_=src, allow_slow_non_contiguous=True))
                for m in mms[:-1]:
                    m.then_inc(P.sems["gq"], 16)
                return mms[-1]
            idxg_ld = P.op("gq", [ag_cc, idxg_ms], idxg_ld_fn, n_dma=16)

            pt_ops = []
            for g in range(8):
                pt_ops.append(P.op("v", [idxg_ld, ld_iotal], lambda e, g=g:
                                   e.tensor_scalar(pt[g][:], iotal_s[:], idxg_f[:, g:g + 1],
                                                   None, op0=ALU.is_equal)))

            rg_ops = [None] * 8
            prev_sog = None
            for g in range(8):
                blk, pp = g // 4, g % 4
                ld_sog = P.op("gq", [ag_cc, prev_sog], lambda e, blk=blk, pp=pp:
                              e.dma_start(out=sog[:],
                                          in_=ag_out[blk:blk + 1, pp * 8192:(pp + 1) * 8192]
                                          .rearrange("o (p2 d) -> (o p2) d", d=64)))
                trT = P.op("t", [ld_sog, ld_identf, prev_sog], lambda e:
                           e.transpose(psTf2[0:64, :], sog[:], identf_s[:]))
                trEv = P.op("v", [trT], lambda e:
                            e.tensor_copy(out=soT[:], in_=psTf2[0:64, :]))
                rg_ev = [None, None]
                for ns in range(2):
                    deps_mm = [trEv] + ([ld_wo[-1]] if ns == 0 else [rg_ev[0]])

                    def mk_rmm(g=g, ns=ns):
                        def f(e):
                            mm = None
                            for hh in range(2):
                                gh = 2 * g + hh
                                mm = e.matmul(psR[64 * hh:64 * hh + 64, :],
                                              soT[:, 64 * hh:64 * hh + 64],
                                              wo_s[:, gh * 1024 + ns * 512:
                                                   gh * 1024 + (ns + 1) * 512],
                                              start=True, stop=True)
                            return mm
                        return f
                    rmm = P.op("t", deps_mm, mk_rmm())
                    rg_ev[ns] = P.op("v", [rmm], lambda e, g=g, ns=ns:
                                     e.tensor_copy(out=rg[g][:, ns * 512:(ns + 1) * 512],
                                                   in_=psR[:, :]))
                rg_ops[g] = rg_ev[1]
                prev_sog = rg_ev[1]

            cmx_ops = [None] * NT2
            x1_st = [None] * NT2
            xl2_st = [None] * NT2
            ln2_st1 = [None] * NT2
            for tb in range(NT2):
                bufx, bufx1, bufl2 = xab2[tb % 2], x1b2[tb % 2], xl2b[tb % 2]
                ldx = P.op("sq", [cmx_ops[tb - 2] if tb >= 2 else None],
                           lambda e, tb=tb, bufx=bufx:
                           e.dma_start(out=bufx[:], in_=x_half[tb * 128:(tb + 1) * 128, :]))
                mm_deps = [pt_ops[-1], rg_ops[-1], ld_bo, ld_ones]
                if tb >= 2:
                    mm_deps.append(x1_st[tb - 2])

                def mk_sct(tb=tb):
                    def f(e):
                        mm = None
                        for ns in range(2):
                            bank = (tb % 2) * 2 + ns
                            for g in range(8):
                                e.matmul(psX[bank][:, :], pt[g][:, tb * 128:(tb + 1) * 128],
                                         rg[g][:, ns * 512:(ns + 1) * 512],
                                         start=(g == 0), stop=False)
                            mm = e.matmul(psX[bank][:, :], onesb_s[:, :],
                                          bo_s[:, ns * 512:(ns + 1) * 512],
                                          start=False, stop=True)
                        return mm
                    return f
                sct = P.op("t", mm_deps, mk_sct())

                def mk_x1(tb=tb, bufx=bufx, bufx1=bufx1):
                    def f(e):
                        for ns in range(2):
                            mm = e.tensor_add(bufx1[:, ns * 512:(ns + 1) * 512],
                                              psX[(tb % 2) * 2 + ns][:, :],
                                              bufx[:, ns * 512:(ns + 1) * 512])
                        return mm
                    return f
                x1v = P.op("v", [sct, ldx], mk_x1())
                x1_st[tb] = P.op("gq", [x1v], lambda e, tb=tb, bufx1=bufx1:
                                 e.dma_start(out=x1_d[tb * 128:(tb + 1) * 128, :],
                                             in_=bufx1[:]))
                sq1 = P.op("s", [x1v, ln2_st1[tb - 1] if tb >= 1 else None],
                           lambda e, bufx1=bufx1:
                           e.activation(out=junk2[:], in_=bufx1[:], func=AF.Square,
                                        accum_out=lnt[:, 1:2]))
                st1 = P.op("v", [x1v, sq1], lambda e, bufx1=bufx1: (
                    e.tensor_reduce(out=lnt[:, 0:1], in_=bufx1[:], axis=AX.X, op=ALU.add),
                    e.drain(),
                    e.tensor_scalar(lnt[:, 2:3], lnt[:, 0:1], 1.0 / D, None, op0=ALU.mult),
                    e.drain(),
                    e.scalar_tensor_tensor(out=lnt[:, 5:6], in0=lnt[:, 2:3],
                                           scalar=lnt[:, 2:3], in1=eps_s[:],
                                           op0=ALU.mult, op1=ALU.subtract),
                    e.drain(),
                    e.tensor_scalar(lnt[:, 3:4], lnt[:, 1:2], 1.0 / D, lnt[:, 5:6],
                                    op0=ALU.mult, op1=ALU.subtract))[-1])
                ln2_st1[tb] = st1
                sq2 = P.op("s", [st1], lambda e:
                           e.activation(out=lnt[:, 4:5], in_=lnt[:, 3:4], func=AF.Sqrt))
                fin = P.op("v", [sq2, x1_st[tb]], lambda e, bufx1=bufx1, bufl2=bufl2: (
                    e.reciprocal(out=lnt[:, 6:7], in_=lnt[:, 4:5]),
                    e.drain(),
                    e.tensor_scalar(lnt[:, 7:8], lnt[:, 2:3], lnt[:, 6:7], -1.0,
                                    op0=ALU.mult, op1=ALU.mult),
                    e.drain(),
                    e.scalar_tensor_tensor(out=bufl2[:], in0=bufx1[:], scalar=lnt[:, 6:7],
                                           in1=lnt[:, 7:8].to_broadcast([128, D]),
                                           op0=ALU.mult, op1=ALU.add))[-1])
                cmx_ops[tb] = P.op("v", [fin], lambda e, tb=tb, bufl2=bufl2:
                                   e.tensor_reduce(out=cmx[:, tb:tb + 1], in_=bufl2[:],
                                                   axis=AX.X, op=ALU.max,
                                                   apply_absolute_value=True))
                xl2_st[tb] = P.op("gq", [cmx_ops[tb]], lambda e, tb=tb, bufl2=bufl2:
                                  e.dma_start(out=xln2_d[tb * 128:(tb + 1) * 128, :],
                                              in_=bufl2[:]))

            vmx = P.op("v", [cmx_ops[-1]], lambda e: (
                e.tensor_reduce(out=mxv[:], in_=cmx[:, 0:NT2], axis=AX.X, op=ALU.max),
                e.drain(), e.nop())[-1])
            tmx = P.op("t", [vmx, ld_identf], lambda e:
                       e.transpose(psTf2[0:1, :], mxv[:], identf_s[:]))
            vmx2 = P.op("v", [tmx], lambda e: (
                e.tensor_reduce(out=mxv[0:1, 0:1], in_=psTf2[0:1, :], axis=AX.X, op=ALU.max),
                e.drain(),
                e.tensor_copy(out=mxrow[:], in_=mxv[0:1, 0:1].to_broadcast([1, 2048])))[-1])
            mx1_st = P.op("gq", [vmx2], lambda e: e.dma_start(out=mx1_in[:, :], in_=mxrow[:]))
            if _os.environ.get("MK_NO_CC"):
                mx1_cc = P.op("g", [mx1_st], lambda e: e.engine_nop())
            else:
                mx1_cc = P.op("g", [mx1_st], lambda e: e.collective_compute(
                    "AllReduce", ALU.max, replica_groups=[[0, 1, 2, 3, 4, 5, 6, 7]],
                    ins=[mx1_in[:, :]], outs=[mx1_out[:, :]]))
            mx1_ld = P.op("gq", [mx1_cc], lambda e: e.dma_start(
                out=mxg[:, 0:16], in_=mx1_out[:, :].rearrange("o (p2 f) -> (o p2) f", f=16)))
            sigx = P.op("s", [ld_clip], lambda e:
                        e.activation(out=scr[:, 0:1], in_=clip_s[:, 0:1], func=AF.Sigmoid))
            scx = P.op("v", [sigx, mx1_ld, wq_v[-1]], lambda e: (
                e.tensor_tensor(scr[:, 1:2], scr[:, 0:1], mxg[:, 0:1], op=ALU.mult),
                e.drain(),
                e.tensor_scalar(scr[:, 2:3], scr[:, 1:2], 1.0 / 127.0, None, op0=ALU.mult),
                e.drain(),
                e.reciprocal(out=rsx[:], in_=scr[:, 2:3]),
                e.tensor_scalar(sxsg[:, :], sgw[:, :], scr[:, 2:3], None, op0=ALU.mult),
                e.tensor_scalar(sxsu[:, :], suw[:, :], scr[:, 2:3], None, op0=ALU.mult),
                e.drain(),
                e.nop())[-1])

            xq_st = [None] * NT2
            for tb in range(NT2):
                bufl2 = xl2b[tb % 2]
                deps = [xl2_st[tb], scx]
                if tb >= 2:
                    deps.append(xq_st[tb - 2])
                ldl2 = P.op("sq", deps, lambda e, tb=tb, bufl2=bufl2:
                            e.dma_start(out=bufl2[:], in_=xln2_d[tb * 128:(tb + 1) * 128, :]))

                def mk_xq(tb=tb, bufl2=bufl2):
                    def f(e):
                        e.tensor_scalar(xqt[:], bufl2[:], rsx[:], CHI,
                                        op0=ALU.mult, op1=ALU.min)
                        e.drain()
                        e.tensor_scalar(xqt[:], xqt[:], CLO, None, op0=ALU.max)
                        e.drain()
                        e.tensor_copy(out=xqi[:], in_=xqt[:])
                        e.drain()
                        return e.tensor_copy(out=xqo[tb % 2][:], in_=xqi[:])
                    return f
                xqv = P.op("v", [ldl2], mk_xq())
                xq_st[tb] = P.op("gq", [xqv], lambda e, tb=tb:
                                 e.dma_start(out=xq_d[tb * 128:(tb + 1) * 128, :],
                                             in_=xqo[tb % 2][:]))

        if _MAXP < 8:
            P.op("v", [xq_st[-1], scx], lambda e: e.nop())
            P.finalize(block)
            return nc
        P.barrier()
        p_es = contextlib.ExitStack()
        PSB = lambda n, s, d: p_es.enter_context(nc.sbuf_tensor(n, s, d))
        PPS = lambda n, s, d: p_es.enter_context(nc.psum_tensor(n, s, d))
        with p_es:
            xqfm = [PSB(f"xqfm{i}", [128, TH], BF16) for i in range(8)]
            wgT = [PSB(f"wgT{i}", [128, II], BF16) for i in range(8)]
            wuT = [PSB(f"wuT{i}", [128, II], BF16) for i in range(8)]
            sgbuf = [PSB(f"sgbuf{i}", [128, 512], F32) for i in range(2)]
            stile = [PSB(f"stile{i}", [128, 512], BF16) for i in range(2)]
            smxc = PSB("smxc", [128, 176], F32)
            mxv2 = PSB("mxv2", [128, 1], F32)
            psG = [PPS(f"psG{i}", [128, 512], F32) for i in range(4)]
            psU = [PPS(f"psU{i}", [128, 512], F32) for i in range(4)]

            xqfm_ld = [P.op("aq", [xq_st[-1]], lambda e, st=st:
                            e.dma_start(out=xqfm[st][:], in_=xq_d[:, st * 128:(st + 1) * 128],
                                        transpose=True)) for st in range(8)]
            wgT_ld = [P.op("aq", [wg_stored], lambda e, st=st:
                           e.dma_start(out=wgT[st][:], in_=wgq_d[:, st * 128:(st + 1) * 128],
                                       transpose=True)) for st in range(8)]
            wuT_ld = [P.op("aq", [wu_stored], lambda e, st=st:
                           e.dma_start(out=wuT[st][:], in_=wuq_d[:, st * 128:(st + 1) * 128],
                                       transpose=True)) for st in range(8)]

            smx_last = None
            s_st_last = None
            prev_sil = [None, None]
            prev_sv = {}
            prev_item_store = {}
            n_item = 0
            for ic in range(NIC):
                for half in range(2):
                    mm_deps = [xqfm_ld[-1], wgT_ld[-1], wuT_ld[-1]]
                    if n_item >= 1:
                        mm_deps.append(prev_item_store[n_item - 1])

                    def mk_gu(ic=ic, half=half):
                        def f(e):
                            for hc in range(8):
                                for ts in range(4):
                                    tsg = half * 4 + ts
                                    e.matmul(psG[ts][:, :],
                                             wgT[hc][:, ic * 128:(ic + 1) * 128],
                                             xqfm[hc][:, tsg * 512:(tsg + 1) * 512],
                                             start=(hc == 0), stop=(hc == 7))
                                for ts in range(4):
                                    tsg = half * 4 + ts
                                    mm = e.matmul(psU[ts][:, :],
                                                  wuT[hc][:, ic * 128:(ic + 1) * 128],
                                                  xqfm[hc][:, tsg * 512:(tsg + 1) * 512],
                                                  start=(hc == 0), stop=(hc == 7))
                            return mm
                        return f
                    gu = P.op("t", mm_deps, mk_gu())
                    last_store = None
                    for ts in range(4):
                        tsg = half * 4 + ts
                        par = (n_item * 4 + ts) % 2
                        sil = P.op("s", [gu, scx,
                                         prev_sv.get((n_item * 4 + ts) - 2)],
                                   lambda e, ic=ic, ts=ts, par=par:
                                   e.activation(out=sgbuf[par][:], in_=psG[ts][:, :],
                                                func=AF.Silu, scale=sxsg[:, ic:ic + 1]))
                        sv = P.op("v", [sil], lambda e, ic=ic, ts=ts, par=par:
                                  e.scalar_tensor_tensor(out=stile[par][:],
                                                         in0=psU[ts][:, :],
                                                         scalar=sxsu[:, ic:ic + 1],
                                                         in1=sgbuf[par][:],
                                                         op0=ALU.mult, op1=ALU.mult))
                        smx = P.op("v", [sv], lambda e, n=n_item, ts=ts, par=par:
                                   e.tensor_reduce(out=smxc[:, (n * 4 + ts) % 176:
                                                            (n * 4 + ts) % 176 + 1],
                                                   in_=stile[par][:], axis=AX.X,
                                                   op=ALU.max, apply_absolute_value=True))
                        sst = P.op("gq", [smx], lambda e, ic=ic, tsg=tsg, par=par:
                                   e.dma_start(out=s_d[ic * 128:(ic + 1) * 128,
                                                       tsg * 512:(tsg + 1) * 512],
                                               in_=stile[par][:]))
                        prev_sv[n_item * 4 + ts] = sst
                        last_store = sst
                        smx_last = smx
                    prev_item_store[n_item] = last_store
                    n_item += 1
            s_st_last = last_store

            vmxs = P.op("v", [smx_last], lambda e: (
                e.tensor_reduce(out=mxv2[:], in_=smxc[:, :], axis=AX.X, op=ALU.max),
                e.drain(), e.nop())[-1])
            tmxs = P.op("t", [vmxs, ld_identf], lambda e:
                        e.transpose(psG[0][0:1, 0:128], mxv2[:], identf_s[:]))
            vmxs2 = P.op("v", [tmxs], lambda e: (
                e.tensor_reduce(out=mxv2[0:1, 0:1], in_=psG[0][0:1, 0:128], axis=AX.X, op=ALU.max),
                e.drain(),
                e.tensor_copy(out=mxrow[:], in_=mxv2[0:1, 0:1].to_broadcast([1, 2048])))[-1])
            mx2_st = P.op("gq", [vmxs2], lambda e: e.dma_start(out=mx2_in[:, :], in_=mxrow[:]))
            if _os.environ.get("MK_NO_CC"):
                mx2_cc = P.op("g", [mx2_st], lambda e: e.engine_nop())
            else:
                mx2_cc = P.op("g", [mx2_st], lambda e: e.collective_compute(
                    "AllReduce", ALU.max, replica_groups=[[0, 1, 2, 3, 4, 5, 6, 7]],
                    ins=[mx2_in[:, :]], outs=[mx2_out[:, :]]))
            mx2_ld = P.op("gq", [mx2_cc], lambda e: e.dma_start(
                out=mxg[:, 0:16], in_=mx2_out[:, :].rearrange("o (p2 f) -> (o p2) f", f=16)))
            sigh = P.op("s", [mx2_ld, ld_clip], lambda e:
                        e.activation(out=scr[:, 0:1], in_=clip_s[:, 1:2], func=AF.Sigmoid))
            sch = P.op("v", [sigh, mx2_ld], lambda e: (
                e.tensor_tensor(scr[:, 1:2], scr[:, 0:1], mxg[:, 0:1], op=ALU.mult),
                e.drain(),
                e.tensor_scalar(shv[:], scr[:, 1:2], 1.0 / 127.0, None, op0=ALU.mult),
                e.drain(),
                e.reciprocal(out=rsh[:], in_=shv[:]),
                e.drain(),
                e.nop())[-1])

        if _MAXP < 9:
            P.op("v", [sch, s_st_last], lambda e: e.nop())
            P.finalize(block)
            return nc
        P.barrier()
        p_es = contextlib.ExitStack()
        PSB = lambda n, s, d: p_es.enter_context(nc.sbuf_tensor(n, s, d))
        PPS = lambda n, s, d: p_es.enter_context(nc.psum_tensor(n, s, d))
        with p_es:
            wdT = [PSB(f"wdT{i}", [128, D], BF16) for i in range(NIC)]
            hq = [PSB(f"hq{i}", [128, 512], BF16) for i in range(NIC)]
            sld = [PSB(f"sld{i}", [128, 512], BF16) for i in range(NIC)]
            qtmp = PSB("qtmp", [128, 512], F32)
            qtmpi = PSB("qtmpi", [128, 512], I32d)
            x1r = [PSB(f"x1r_{i}", [128, D], F32) for i in range(2)]
            outb = [PSB(f"out_{i}", [128, D], F32) for i in range(2)]
            psD = [PPS(f"psD{i}", [128, 512], F32) for i in range(8)]

            if "hq_dbg" in debug:
                P.op("gq", [sch], lambda e: (
                    e.dma_start(out=sh_dbg[:, 0:1], in_=shv[:], allow_slow_non_contiguous=True).then_inc(P.sems["gq"], 16),
                    e.dma_start(out=sh_dbg[:, 1:2], in_=rsh[:], allow_slow_non_contiguous=True))[-1], n_dma=2)
            wdT_ld = [P.op("aq", [wd_stored], lambda e, ic=ic:
                           e.dma_start(out=wdT[ic][:], in_=wdq_d[:, ic * 128:(ic + 1) * 128],
                                       transpose=True)) for ic in range(NIC)]
            if "hq_dbg" in debug:
                for ic in range(NIC):
                    P.op("gq", [wdT_ld[ic]], lambda e, ic=ic:
                         e.dma_start(out=wdt_dbg[ic * 128:(ic + 1) * 128, :], in_=wdT[ic][:]))
            out_st = []
            prev_hq_last = None
            prev_mm_last = None
            prev_ev = {}
            for ts in range(8):
                hq_ops = []
                for ic in range(NIC):
                    ld_s = P.op("sq", [s_st_last if ts == 0 else None,
                                       prev_hq_last if ts >= 1 else None],
                                lambda e, ic=ic, ts=ts:
                                e.dma_start(out=sld[ic][:],
                                            in_=s_d[ic * 128:(ic + 1) * 128,
                                                    ts * 512:(ts + 1) * 512]))
                    deps_h = [ld_s, sch]
                    if ts >= 1:
                        deps_h.append(prev_mm_last)

                    def mk_hq(ic=ic):
                        def f(e):
                            e.tensor_scalar(qtmp[:], sld[ic][:], rsh[:], CHI,
                                            op0=ALU.mult, op1=ALU.min)
                            e.drain()
                            e.tensor_scalar(qtmp[:], qtmp[:], CLO, None, op0=ALU.max)
                            e.drain()
                            e.tensor_copy(out=qtmpi[:], in_=qtmp[:])
                            e.drain()
                            return e.tensor_copy(out=hq[ic][:], in_=qtmpi[:])
                        return f
                    hq_ops.append(P.op("v", deps_h, mk_hq()))
                    if ts == 0 and "hq_dbg" in debug:
                        P.op("gq", [hq_ops[-1]], lambda e, ic=ic:
                             e.dma_start(out=hq_dbg[ic * 128:(ic + 1) * 128, :],
                                         in_=hq[ic][:]))
                prev_hq_last = hq_ops[-1]
                mm_last = None
                for tb in range(4):
                    dep_x1 = prev_ev.get((ts, tb - 2)) if tb >= 2 else \
                        prev_ev.get((ts - 1, tb + 2))
                    ldx1 = P.op("sq", [dep_x1],
                                lambda e, ts=ts, tb=tb:
                                e.dma_start(out=x1r[tb % 2][:],
                                            in_=x1_d[ts * 512 + tb * 128:
                                                     ts * 512 + (tb + 1) * 128, :]))
                    for hn in range(2):
                        deps_mm = [hq_ops[-1], wdT_ld[-1]]
                        if ts >= 1:
                            deps_mm.append(prev_ev.get((ts - 1, tb)))

                        def mk_dn(tb=tb, hn=hn):
                            def f(e):
                                for ic in range(NIC):
                                    mm = e.matmul(psD[tb * 2 + hn][:, :],
                                                  hq[ic][:, tb * 128:(tb + 1) * 128],
                                                  wdT[ic][:, hn * 512:(hn + 1) * 512],
                                                  start=(ic == 0), stop=(ic == NIC - 1))
                                return mm
                            return f
                        mm_last = P.op("t", deps_mm, mk_dn())
                        if hn == 0:
                            mm0 = mm_last

                    def mk_ev(ts=ts, tb=tb):
                        def f(e):
                            for hn in range(2):
                                mm = e.scalar_tensor_tensor(
                                    out=outb[tb % 2][:, hn * 512:(hn + 1) * 512],
                                    in0=psD[tb * 2 + hn][:, :], scalar=shv[:],
                                    in1=x1r[tb % 2][:, hn * 512:(hn + 1) * 512],
                                    op0=ALU.mult, op1=ALU.add)
                            return mm
                        return f
                    ev = P.op("v", [mm_last, ldx1, out_st[-2] if len(out_st) >= 2 else None],
                              mk_ev())
                    prev_ev[(ts, tb)] = ev
                    out_st.append(P.op("gq", [ev], lambda e, ts=ts, tb=tb:
                                       e.dma_start(out=y[ts * 512 + tb * 128:
                                                         ts * 512 + (tb + 1) * 128, :],
                                                   in_=outb[tb % 2][:])))
                prev_mm_last = mm_last

            P.op("v", [out_st[-1]], lambda e: e.nop())

        P.finalize(block)
    return nc


def make_in_maps(inputs):
    x = np.ascontiguousarray(np.asarray(inputs["L0_attn_x"], dtype=np.float32))
    Wq = np.asarray(inputs["L0_attn_Wq"], np.float32)
    Wk = np.asarray(inputs["L0_attn_Wk"], np.float32)
    Wv = np.asarray(inputs["L0_attn_Wv"], np.float32)
    Wo = np.asarray(inputs["L0_attn_Wo"], np.float32)
    bq = np.asarray(inputs["L0_attn_bq"], np.float32)
    bk = np.asarray(inputs["L0_attn_bk"], np.float32)
    bv = np.asarray(inputs["L0_attn_bv"], np.float32)
    bo = np.asarray(inputs["L0_attn_bo"], np.float32)
    clip_x = np.asarray(inputs["L0_ffn_clip_x"], np.float32)
    clip_down = np.asarray(inputs["L0_ffn_clip_down"], np.float32)
    wg = np.asarray(inputs["L0_ffn_w_gate"], np.float32)
    wu = np.asarray(inputs["L0_ffn_w_up"], np.float32)
    wd = np.asarray(inputs["L0_ffn_w_down"], np.float32)

    ident_b = np.eye(128, dtype=ml_dtypes.bfloat16)
    ident_f = np.eye(128, dtype=np.float32)
    ones_b = np.ones((1, 128), dtype=ml_dtypes.bfloat16)
    clip_rep = np.repeat(np.array([[clip_x[0], clip_down[0]]], np.float32), 128, axis=0)
    iota_full = np.repeat(np.arange(T, dtype=np.float32)[None, :], 128, axis=0)
    in_maps = []
    for c in range(NCORES):
        b, hg = c // 2, c % 2
        hs = slice(hg * 512, (hg + 1) * 512)
        off = hg * TH
        iota_l = np.repeat(np.arange(off, off + TH, dtype=np.float32)[None, :], 128, axis=0)
        in_maps.append({
            "x_full": x[b],
            "x_half": np.ascontiguousarray(x[b, off:off + TH]),
            "wq": np.ascontiguousarray(Wq[:, hs]),
            "wk": np.ascontiguousarray(Wk[:, hs]),
            "wv": np.ascontiguousarray(Wv[:, hs]),
            "wo": Wo,
            "bq_r": np.repeat(bq[None, hs], 128, axis=0),
            "bv_r": np.repeat(bv[None, hs], 128, axis=0),
            "bk_c": np.ascontiguousarray(bk[hs].reshape(4, 128).T),
            "bo_b": bo[None, :],
            "wg": wg, "wu": wu, "wd": wd,
            "clip_r": clip_rep,
            "iota_f": iota_full,
            "iota_loc": iota_l,
            "ident_b": ident_b,
            "ident_f": ident_f,
            "ones_b": ones_b,
        })
    return in_maps


def kernel(**inputs):
    in_maps = make_in_maps(inputs)
    nc = build_nc()
    res = run_bass_kernel_spmd(nc, in_maps, core_ids=list(range(NCORES)))
    out = np.empty((B, T, D), np.float32)
    for c in range(NCORES):
        b, th = c // 2, c % 2
        out[b, th * TH:(th + 1) * TH] = res.results[c]["y"]
    return out
